# version 25
# baseline (speedup 1.0000x reference)
import contextlib
import numpy as np
import ml_dtypes
import concourse.bass as bass
import concourse.mybir as mybir
from concourse.bass_utils import run_bass_kernel_spmd

F32 = mybir.dt.float32
BF16 = mybir.dt.bfloat16
AF = mybir.ActivationFunctionType
ALU = mybir.AluOpType
AX = mybir.AxisListType

MAGIC = 1.5 * 2.0 ** 23
EPS = 1e-5
INV127 = 1.0 / 127.0
INV1024 = 1.0 / 1024.0

ENGS = ("sync", "vector", "scalar", "tensor", "gpsimd")

B_CORE = 2
S = 2048
T = B_CORE * S
H = 1024
HK = H // 128
NT = T // 128
NTB = S // 128
I_DIM = 2816
IT = I_DIM // 128
NTC = T // 512
NQC = S // 512
NHEADS = 16
N_CORES = 8


class Sched:
    def __init__(self, nc, st):
        self.nc = nc
        self.st = st
        self.ops = {e: [] for e in ENGS}
        self.sems = {}
        self.cnt = {}
        self.hw = {e: {} for e in ENGS}

    def sem(self, name):
        if name not in self.sems:
            self.sems[name] = self.st.enter_context(self.nc.semaphore(name))
            self.cnt[name] = 0
        return name

    def op(self, eng, fn, waits=(), inc=None, n=1):
        pruned = []
        for s, v in waits:
            if v > self.hw[eng].get(s, 0):
                pruned.append((s, v))
                self.hw[eng][s] = v
        self.ops[eng].append((fn, pruned, inc, n))
        if inc is not None:
            self.cnt[inc] += n
            return self.cnt[inc]
        return None

    def dma(self, eng, out, in_, sem, waits=(), gate=False):
        if gate and self.cnt.get(sem, 0):
            waits = list(waits) + [(sem, self.cnt[sem])]
        return self.op(eng, lambda e, o=out, i=in_: e.dma_start(out=o, in_=i),
                       waits, inc=sem, n=16)

    def wait(self, eng, waits):
        self.op(eng, None, waits)

    def join(self):
        for eng in ENGS:
            for s, c in self.cnt.items():
                if c > 0:
                    self.wait(eng, [(s, c)])

    def realize(self, blk):
        sems = self.sems
        for eng in ENGS:
            ops = self.ops[eng]

            def body(e, ops=ops):
                for fn, waits, inc, n in ops:
                    for s, v in waits:
                        e.wait_ge(sems[s], v)
                    if fn is not None:
                        ins = fn(e)
                        if inc is not None:
                            ins.then_inc(sems[inc], n)
            getattr(blk, eng)(body)


def build_program(name="mkmac"):
    nc = bass.Bass("TRN2", target_bir_lowering=False, name=name)

    d_x = nc.dram_tensor("x", [T, H], F32, kind="ExternalInput").ap()
    d_wqkv = nc.dram_tensor("wqkv", [H, 3 * H], F32, kind="ExternalInput").ap()
    d_wout = nc.dram_tensor("wout", [H, H], F32, kind="ExternalInput").ap()
    d_wg = nc.dram_tensor("wg", [I_DIM, H], F32, kind="ExternalInput").ap()
    d_wu = nc.dram_tensor("wu", [I_DIM, H], F32, kind="ExternalInput").ap()
    d_wd = nc.dram_tensor("wd", [H, I_DIM], F32, kind="ExternalInput").ap()
    d_clipx = nc.dram_tensor("clipx", [1, 1], F32, kind="ExternalInput").ap()
    d_clipd = nc.dram_tensor("clipd", [1, 1], F32, kind="ExternalInput").ap()
    d_o = nc.dram_tensor("o", [T, H], F32, kind="ExternalOutput").ap()

    d_xres = nc.dram_tensor("xres", [T, H], F32, kind="Internal").ap()
    d_h = nc.dram_tensor("hbuf", [I_DIM, T], BF16, kind="Internal").ap()
    d_gsp = nc.dram_tensor("gsp", [I_DIM * HK, 128], BF16, kind="Internal").ap()
    d_usp = nc.dram_tensor("usp", [I_DIM * HK, 128], BF16, kind="Internal").ap()
    d_dsp = nc.dram_tensor("dsp", [I_DIM, H], BF16, kind="Internal").ap()
    d_qksp = nc.dram_tensor("qksp", [16 * HK * 128, 128], BF16,
                            kind="Internal").ap()
    d_vsp = nc.dram_tensor("vsp", [2 * HK * 128, 512], BF16,
                           kind="Internal").ap()
    d_cc = [(nc.dram_tensor(f"ccin{i}", [1, 128], F32, kind="Internal").ap(),
             nc.dram_tensor(f"ccout{i}", [1, 128], F32, kind="Internal").ap())
            for i in range(4)]

    st = contextlib.ExitStack()
    with st:
        K = Sched(nc, st)

        def sb(n, sh, dt, stk=None):
            return (stk or st).enter_context(nc.sbuf_tensor(n, sh, dt))

        ident = sb("ident", [128, 128], BF16)
        ones_row = sb("ones_row", [1, 128], F32)
        ones2 = sb("ones2", [128, 2], BF16)
        swg = sb("swg", [128, IT], F32)
        swu = sb("swu", [128, IT], F32)
        swxg = sb("swxg", [128, IT], F32)
        swxu = sb("swxu", [128, IT], F32)
        S_a = sb("S_a", [128, NT], F32)
        Q_a = sb("Q_a", [128, NT], F32)
        Mx_a = sb("Mx_a", [128, NT], F32)
        Mn_a = sb("Mn_a", [128, NT], F32)
        m_a = sb("m_a", [128, NT], F32)
        r_a = sb("r_a", [128, NT], F32)
        A_a = sb("A_a", [128, NT], F32)
        B_a = sb("B_a", [128, NT], F32)
        var_a = sb("var_a", [128, NT], F32)
        tmp_a = sb("tmp_a", [128, NT], F32)
        hmax_a = sb("hmax_a", [128, IT * NTC], F32)
        colmax = sb("colmax", [128, 1], F32)
        rowmax = sb("rowmax", [1, 128], F32)
        ccrow = sb("ccrow", [1, 128], F32)
        gmaxr = sb("gmaxr", [1, 128], F32)
        sc = sb("sc", [1, 16], F32)
        clip_s = sb("clip_s", [1, 2], F32)
        bc1 = sb("bc1", [128, 1], F32)
        bc2 = sb("bc2", [128, 1], F32)
        bc3 = sb("bc3", [128, 1], F32)
        bc4 = sb("bc4", [128, 1], F32)
        negB = sb("negB", [128, NHEADS], F32)
        xs = (sb("xs0", [128, H], F32), sb("xs1", [128, H], F32))
        xn = (sb("xn0", [128, H], F32), sb("xn1", [128, H], F32))
        recb = sb("recb", [1, 512], F32)
        ssqm = sb("ssqm", [2, 64], F32)
        ssqr = sb("ssqr", [2, 16], F32)
        negBr = sb("negBr", [1, 16], F32)
        eps_c = sb("eps_c", [128, 1], F32)
        nmag_c = sb("nmag_c", [128, 1], F32)

        pA = st.enter_context(nc.psum_tensor("pA", [128, 1024], F32))
        pB = st.enter_context(nc.psum_tensor("pB", [128, 1024], F32))
        pC = st.enter_context(nc.psum_tensor("pC", [128, 512], F32))
        pD = st.enter_context(nc.psum_tensor("pD", [128, 512], F32))
        pE = st.enter_context(nc.psum_tensor("pE", [128, 512], F32))
        ptr = st.enter_context(nc.psum_tensor("ptr", [128, 1024], BF16))
        pCD = (pC, pD)

        SV = K.sem("SV")
        SA = K.sem("SA")
        SP = K.sem("SP")
        SG = K.sem("SG")
        SX = [K.sem("SX0"), K.sem("SX1")]
        SWL = K.sem("SWL")
        SHW = K.sem("SHW")
        SHF4 = [K.sem("SHF0"), K.sem("SHF1"), K.sem("SHF2"), K.sem("SHF3")]
        SHWF = [K.sem("SHF4"), K.sem("SHF5"), K.sem("SHF6"), K.sem("SHF7")]
        SWG = [K.sem("SWG0"), K.sem("SWG1")]
        SPL = [K.sem("SPL0"), K.sem("SPL1")]
        SXG = [K.sem("SXG0"), K.sem("SXG1")]
        SHWs = [K.sem("SHWa"), K.sem("SHWb"), K.sem("SHWc"), K.sem("SHWd")]
        SQK = [K.sem("SQK0"), K.sem("SQK1")]
        SVW = K.sem("SVW")
        SO = [K.sem("SO0"), K.sem("SO1")]
        STD = K.sem("STD")
        SSP = K.sem("SSP")
        SCC = K.sem("SCC")

        cur = {"sqA": None}
        xcons_v = [0, 0]
        xcons_a = [0, 0]
        ocons = [0, 0]

        def v_ts(out, in0, s1, s2, o1, o2=ALU.bypass, waits=(), inc=SV):
            return K.op("vector",
                        lambda e, o=out, i=in0, a=s1, b=s2, x=o1, y=o2:
                        e.tensor_scalar(out=o, in0=i, scalar1=a, scalar2=b,
                                        op0=x, op1=y), waits, inc=inc)

        def v_tt(out, in0, in1, alu, waits=(), inc=SV):
            return K.op("vector",
                        lambda e, o=out, a=in0, b=in1, x=alu:
                        e.tensor_tensor(out=o, in0=a, in1=b, op=x),
                        waits, inc=inc)

        def v_stt(out, in0, scal, in1, o1, o2, waits=(), inc=SV):
            return K.op("vector",
                        lambda e, o=out, a=in0, s=scal, b=in1, x=o1, y=o2:
                        e.scalar_tensor_tensor(out=o, in0=a, scalar=s, in1=b,
                                               op0=x, op1=y), waits, inc=inc)

        def v_red(out, in_, alu, absv=False, waits=(), inc=SV):
            return K.op("vector",
                        lambda e, o=out, i=in_, a=alu, ab=absv:
                        e.tensor_reduce(out=o, in_=i, axis=AX.X, op=a,
                                        apply_absolute_value=ab),
                        waits, inc=inc)

        def v_copy(out, in_, waits=(), inc=SV):
            return K.op("vector",
                        lambda e, o=out, i=in_: e.tensor_copy(out=o, in_=i),
                        waits, inc=inc)

        def v_recip(out, in_, waits=(), inc=SV):
            return K.op("vector",
                        lambda e, o=out, i=in_: e.reciprocal(out=o, in_=i),
                        waits, inc=inc)

        def a_act(out, in_, func, bias=0.0, scale=1.0, accum=None, waits=(),
                  inc=SA):
            return K.op("scalar",
                        lambda e, o=out, i=in_, f=func, b=bias, s=scale,
                        ac=accum: e.activation(out=o, in_=i, func=f, bias=b,
                                               scale=s, accum_out=ac),
                        waits, inc=inc)

        def a_copy(out, in_, waits=(), inc=SA):
            return K.op("scalar",
                        lambda e, o=out, i=in_: e.copy(out=o, in_=i),
                        waits, inc=inc)

        def t_mm(out, lhsT, rhs, start, stop, waits=(), inc="auto"):
            if inc == "auto":
                inc = SP if stop else None
            return K.op("tensor",
                        lambda e, o=out, l=lhsT, r=rhs, s0=start, s1=stop:
                        e.matmul(o, l, r, start=s0, stop=s1), waits, inc=inc)

        def t_tr(out, in_, waits=(), inc=SP):
            return K.op("tensor",
                        lambda e, o=out, i=in_: e.transpose(o, i, ident[:]),
                        waits, inc=inc)

        def bcast128(dst, src11, waits=()):
            t_mm(pE[:, 0:1], ones_row[0:1, 0:128], src11, True, True,
                 waits=list(waits) + [(SA, K.cnt[SA])])
            return a_copy(dst, pE[:, 0:1], waits=[(SP, K.cnt[SP])])

        def sigmoid_via_exp(dst, src11, waits=()):
            a_act(dst, src11, AF.Exp, scale=-1.0, waits=list(waits) +
                  [(SV, K.cnt[SV])])
            v_ts(dst, dst, 1.0, None, ALU.add, waits=[(SA, K.cnt[SA])])
            return v_recip(dst, dst, waits=[(SV, K.cnt[SV])])

        def allreduce_max(src11, idx, waits):
            vb = v_ts(ccrow[0:1, :], src11.broadcast_to((1, 128)), 1.0, None,
                      ALU.mult, waits=list(waits) + [(STD, K.cnt[STD])])
            d1 = K.dma("gpsimd", d_cc[idx][0], ccrow[0:1, :], STD,
                       waits=[(SV, vb)])
            ccv = K.op("gpsimd",
                       lambda e, i=idx: e.collective_compute(
                           "AllReduce", ALU.max,
                           replica_groups=[list(range(N_CORES))],
                           ins=[d_cc[i][0].opt()], outs=[d_cc[i][1].opt()]),
                       waits=[(STD, d1)], inc=SCC, n=1)
            return K.dma("gpsimd", gmaxr[0:1, :], d_cc[idx][1], STD,
                         waits=[(SCC, ccv), (SV, K.cnt[SV])], gate=True)

        def ln_stats_tile(t, buf, ld_sem, ld_cnt, minmax, vsq=False):
            v_red(S_a[:, t:t + 1], buf[:, :], ALU.add,
                  waits=[(ld_sem, ld_cnt)])
            if minmax:
                v_red(Mx_a[:, t:t + 1], buf[:, :], ALU.max)
                v_red(Mn_a[:, t:t + 1], buf[:, :], ALU.min)
            if vsq:
                v_tt(cur["sqA"][:, :], buf[:, :], buf[:, :], ALU.mult,
                     waits=[(ld_sem, ld_cnt), (SV, K.cnt[SV])])
                v_red(Q_a[:, t:t + 1], cur["sqA"][:, :], ALU.add,
                      waits=[(SV, K.cnt[SV])])
            else:
                a_act(cur["sqA"][:, :], buf[:, :], AF.Square,
                      accum=Q_a[:, t:t + 1],
                      waits=[(ld_sem, ld_cnt), (SA, K.cnt[SA])])
            v_ts(m_a[:, t:t + 1], S_a[:, t:t + 1], INV1024, None, ALU.mult,
                 waits=[(SV, K.cnt[SV])])
            v_ts(tmp_a[:, t:t + 1], Q_a[:, t:t + 1], INV1024, None, ALU.mult,
                 waits=[(SA, K.cnt[SA])])
            v_tt(var_a[:, t:t + 1], m_a[:, t:t + 1], m_a[:, t:t + 1],
                 ALU.mult, waits=[(SV, K.cnt[SV])])
            v_tt(var_a[:, t:t + 1], tmp_a[:, t:t + 1], var_a[:, t:t + 1],
                 ALU.subtract, waits=[(SV, K.cnt[SV])])
            a_act(var_a[:, t:t + 1], var_a[:, t:t + 1], AF.Ln,
                  bias=eps_c[:], waits=[(SV, K.cnt[SV])])
            a_act(r_a[:, t:t + 1], var_a[:, t:t + 1], AF.Exp, scale=-0.5,
                  waits=[(SA, K.cnt[SA])])

        def ffn_stage_A(src, t, vsq=False, extraw=()):
            sl = t % 2
            buf = xs[sl]
            waits = list(extraw)
            if xcons_v[sl]:
                waits.append((SV, xcons_v[sl]))
            if xcons_a[sl]:
                waits.append((SA, xcons_a[sl]))
            ld = K.dma("sync", buf[:, :], src[t * 128:(t + 1) * 128, :],
                       SX[sl], waits=waits)
            ln_stats_tile(t, buf, SX[sl], ld, True, vsq=vsq)
            xcons_v[sl] = K.cnt[SV]
            xcons_a[sl] = K.cnt[SA]

        def stats_tail(ar_idx):
            v_tt(Mx_a[:, :], Mx_a[:, :], m_a[:, :], ALU.subtract,
                 waits=[(SV, K.cnt[SV]), (SA, K.cnt[SA])])
            v_tt(Mn_a[:, :], m_a[:, :], Mn_a[:, :], ALU.subtract,
                 waits=[(SV, K.cnt[SV])])
            v_tt(Mx_a[:, :], Mx_a[:, :], Mn_a[:, :], ALU.max,
                 waits=[(SV, K.cnt[SV])])
            v_tt(Mx_a[:, :], Mx_a[:, :], r_a[:, :], ALU.mult,
                 waits=[(SV, K.cnt[SV])])
            v_red(colmax[:], Mx_a[:, :], ALU.max, waits=[(SV, K.cnt[SV])])
            dT = K.dma("gpsimd", rowmax[0:1, :], colmax[:, 0:1], STD,
                       waits=[(SV, K.cnt[SV])], gate=True)
            v_red(sc[0:1, 0:1], rowmax[0:1, :], ALU.max, waits=[(STD, dT)])
            return allreduce_max(sc[0:1, 0:1], ar_idx, [(SV, K.cnt[SV])])


        g = K.op("gpsimd", lambda e: e.memset(ident[:], 0.0), inc=SG)
        g = K.op("gpsimd",
                 lambda e: e.affine_select(out=ident[:], in_=ident[:],
                                           compare_op=ALU.not_equal, fill=1.0,
                                           base=0, pattern=[[-1, 128]],
                                           channel_multiplier=1),
                 waits=[(SG, g)], inc=SG)
        IDENT_SG = g
        K.op("gpsimd", lambda e: e.memset(ones_row[:], 1.0), inc=SG)
        g = K.op("gpsimd", lambda e: e.memset(ones2[:], 0.0), inc=SG)
        g = K.op("gpsimd", lambda e: e.memset(ones2[0:64, 0:1], 1.0),
                 waits=[(SG, g)], inc=SG)
        K.op("gpsimd", lambda e: e.memset(ones2[64:128, 1:2], 1.0),
             waits=[(SG, g)], inc=SG)
        K.op("gpsimd", lambda e: e.memset(eps_c[:], EPS), inc=SG)
        CONST_SG = K.op("gpsimd", lambda e: e.memset(nmag_c[:], -MAGIC),
                        inc=SG)
        K.wait("scalar", [(SG, CONST_SG)])
        K.dma("sync", clip_s[0:1, 0:1], d_clipx, SWL)
        K.dma("sync", clip_s[0:1, 1:2], d_clipd, SWL)

        prep_st = contextlib.ExitStack()
        wld = (sb("wld0", [128, 3 * H], F32, prep_st),
               sb("wld1", [128, 3 * H], F32, prep_st))
        wint = sb("wint", [128, 3 * H], BF16, prep_st)
        wtr = sb("wtr", [128, I_DIM], BF16, prep_st)
        wrow = sb("wrow", [128, 1], F32, prep_st)
        wrcp = sb("wrcp", [128, 1], F32, prep_st)

        cons = [0, 0]
        last_ev = [0]
        last_sp = [0]
        slot_i = [0]
        cur["sqA"] = xn[0]
        K.wait("tensor", [(SG, IDENT_SG)])

        def prep_rowtile(src_ap, width, fold, sw_col, spill_fn, idx):
            sl = slot_i[0] % 2
            slot_i[0] += 1
            buf = wld[sl]
            waits = [(SV, cons[sl])] if cons[sl] else []
            ld = K.dma("sync", buf[:, 0:width], src_ap, SPL[sl], waits=waits)
            v_red(wrow[:], buf[:, 0:width], ALU.max, absv=True,
                  waits=[(SPL[sl], ld), (SV, K.cnt[SV])])
            v_ts(wrow[:], wrow[:], INV127, None, ALU.mult,
                 waits=[(SV, K.cnt[SV])])
            v_recip(wrcp[:], wrow[:], waits=[(SV, K.cnt[SV])])
            v_ts(buf[:, 0:width], buf[:, 0:width], wrcp[:], MAGIC,
                 ALU.mult, ALU.add, waits=[(SV, K.cnt[SV])])
            wi_w = [(SV, K.cnt[SV])]
            if last_sp[0]:
                wi_w.append((SSP, last_sp[0]))
            if fold:
                vq = v_ts(wint[:, 0:width], buf[:, 0:width], MAGIC, wrow[:],
                          ALU.subtract, ALU.mult, waits=wi_w)
            else:
                vq = v_ts(wint[:, 0:width], buf[:, 0:width], MAGIC, None,
                          ALU.subtract, waits=wi_w)
                v_copy(sw_col, wrow[:], waits=[(SV, vq)])
            cons[sl] = K.cnt[SV]
            wt = width // 128
            for c0 in range(0, wt, 8):
                nb = min(8, wt - c0)
                tw = [(SV, vq)]
                if last_ev[0]:
                    tw.append((SV, last_ev[0]))
                for b in range(nb):
                    t_tr(ptr[:, b * 128:(b + 1) * 128],
                         wint[:, (c0 + b) * 128:(c0 + b + 1) * 128],
                         waits=(tw if b == 0 else []),
                         inc=(SP if b == nb - 1 else None))
                last_ev[0] = v_copy(wtr[:, c0 * 128:(c0 + nb) * 128],
                                    ptr[:, 0:nb * 128],
                                    waits=[(SP, K.cnt[SP])])
            last_sp[0] = spill_fn(idx, wt)

        def mk_spill_gu(dst_sp):
            def f(m, wt):
                return K.dma(
                    "gpsimd",
                    dst_sp[m * wt * 128:(m + 1) * wt * 128, :]
                    .rearrange("(k h) c -> h k c", k=wt),
                    wtr[:, 0:wt * 128].rearrange("p (k c) -> p k c", k=wt),
                    SSP, waits=[(SV, last_ev[0])], gate=True)
            return f

        def sp_d(jt, wt):
            return K.dma(
                "gpsimd",
                d_dsp[:, jt * 128:(jt + 1) * 128]
                .rearrange("(k h) c -> h k c", k=IT),
                wtr[:, 0:IT * 128].rearrange("p (k c) -> p k c", k=IT),
                SSP, waits=[(SV, last_ev[0])], gate=True)

        a_tile = [0]

        def emit_A_tiles(n):
            while a_tile[0] < NT and n > 0:
                ffn_stage_A(d_x, a_tile[0])
                a_tile[0] += 1
                n -= 1

        f_g = mk_spill_gu(d_gsp)
        for m in range(IT):
            prep_rowtile(d_wg[m * 128:(m + 1) * 128, :], H, False,
                         swg[:, m:m + 1], f_g, m)
            emit_A_tiles(1)
        f_u = mk_spill_gu(d_usp)
        for m in range(IT):
            prep_rowtile(d_wu[m * 128:(m + 1) * 128, :], H, False,
                         swu[:, m:m + 1], f_u, m)
            emit_A_tiles(1)
        emit_A_tiles(NT)
        pre_d2_0 = stats_tail(0)

        for jt in range(HK):
            prep_rowtile(d_wd[jt * 128:(jt + 1) * 128, :], I_DIM, True,
                         None, sp_d, jt)
        for k in range(HK):
            sl = slot_i[0] % 2
            slot_i[0] += 1
            buf = wld[sl]
            waits = [(SV, cons[sl])] if cons[sl] else []
            ld = K.dma("sync", buf[:, 0:3 * H],
                       d_wqkv[k * 128:(k + 1) * 128, :], SPL[sl], waits=waits)
            wi_w = [(SPL[sl], ld), (SV, K.cnt[SV])]
            if last_sp[0]:
                wi_w.append((SSP, last_sp[0]))
            vq = v_copy(wint[:, 0:3 * H], buf[:, 0:3 * H], waits=wi_w)
            cons[sl] = K.cnt[SV]
            K.dma("gpsimd",
                  d_qksp.rearrange("(c k h) w -> k h c w", c=16, k=HK)[k],
                  wint[:, 0:2048].rearrange("h (c w) -> h c w", w=128),
                  SSP, waits=[(SV, vq)], gate=True)
            last_sp[0] = K.dma(
                "gpsimd",
                d_vsp.rearrange("(jc k h) w -> k h jc w", jc=2, k=HK)[k],
                wint[:, 2048:3072].rearrange("h (jc w) -> h jc w", w=512),
                SSP, waits=[(SV, vq)], gate=True)

        emit_A_tiles(NT)
        K.join()
        prep_st.close()

        def ffn(call, src, dst, skip_A=False, pre_d2=None):
            ffn_st = contextlib.ExitStack()
            actT = sb(f"actT{call}", [128, HK * T], BF16, ffn_st)
            wdq = sb(f"wdq{call}", [128, IT * H], BF16, ffn_st)
            hqs = (sb(f"hqs0_{call}", [128, IT * 512], BF16, ffn_st),
                   sb(f"hqs1_{call}", [128, IT * 512], BF16, ffn_st))
            wgu = (sb(f"wgu0_{call}", [128, 2 * H], BF16, ffn_st),
                   sb(f"wgu1_{call}", [128, 2 * H], BF16, ffn_st))
            gs = (sb(f"gs0_{call}", [128, 512], F32, ffn_st),
                  sb(f"gs1_{call}", [128, 512], F32, ffn_st))
            up = (sb(f"up0_{call}", [128, 512], F32, ffn_st),
                  sb(f"up1_{call}", [128, 512], F32, ffn_st))
            ht = tuple(sb(f"ht{i}_{call}", [128, 512], F32, ffn_st)
                       for i in range(2))
            htb = tuple(sb(f"htb{i}_{call}", [128, 512], BF16, ffn_st)
                        for i in range(4))
            hfb = tuple(sb(f"hfb{i}_{call}", [128, 512], BF16, ffn_st)
                        for i in range(4))
            hfw = sb(f"hfw_{call}", [128, 512], F32, ffn_st)
            cur["sqA"] = xn[0]
            if ocons[0]:
                K.wait("scalar", [(SO[0], ocons[0])])
            if ocons[1]:
                K.wait("scalar", [(SO[1], ocons[1])])
            xqi = hqs[0][:, 0:H]
            actT_v = actT[:, :].rearrange("p (k tt) -> p k tt", k=HK)

            wdq_ld = K.dma("sync",
                           wdq[:, :].rearrange("p (m c) -> p m c", m=IT),
                           d_dsp.rearrange("(m p) c -> p m c", m=IT),
                           SWL, waits=[(SSP, K.cnt[SSP])], gate=True)

            if not skip_A:
                for t in range(NT):
                    ffn_stage_A(src, t)
            if pre_d2 is None:
                d2 = stats_tail(call * 2)
            else:
                d2 = pre_d2
            sigmoid_via_exp(sc[0:1, 1:2], clip_s[0:1, 0:1],
                            waits=[(SWL, K.cnt[SWL])])
            v_tt(sc[0:1, 2:3], sc[0:1, 1:2], gmaxr[0:1, 0:1], ALU.mult,
                 waits=[(STD, d2), (SV, K.cnt[SV])])
            v_recip(sc[0:1, 3:4], sc[0:1, 2:3], waits=[(SV, K.cnt[SV])])
            v_ts(sc[0:1, 3:4], sc[0:1, 3:4], 127.0, None, ALU.mult,
                 waits=[(SV, K.cnt[SV])])
            v_ts(sc[0:1, 4:5], sc[0:1, 2:3], INV127, None, ALU.mult,
                 waits=[(SV, K.cnt[SV])])
            bcast128(bc1[:], sc[0:1, 3:4], waits=[(SV, K.cnt[SV])])
            bcast128(bc2[:], sc[0:1, 4:5])
            acd = K.cnt[SA]
            v_ts(A_a[:, :], r_a[:, :], bc1[:], None, ALU.mult,
                 waits=[(SA, acd)])
            v_stt(B_a[:, :], m_a[:, :], -1.0, A_a[:, :], ALU.mult, ALU.mult,
                  waits=[(SV, K.cnt[SV])])
            v_ts(swxg[:, :], swg[:, :], bc2[:], None, ALU.mult,
                 waits=[(SA, acd)])
            v_ts(swxu[:, :], swu[:, :], bc2[:], None, ALU.mult)
            ab_done = K.cnt[SV]

            tr_ev = 0
            for t in range(NT):
                sl = t % 2
                buf = xs[sl]
                ld = K.dma("sync", buf[:, :], src[t * 128:(t + 1) * 128, :],
                           SX[sl], waits=[(SV, xcons_v[sl])])
                v_ts(buf[:, :], buf[:, :], A_a[:, t:t + 1], B_a[:, t:t + 1],
                     ALU.mult, ALU.add, waits=[(SX[sl], ld), (SV, ab_done)])
                v_ts(buf[:, :], buf[:, :], MAGIC, MAGIC, ALU.add,
                     ALU.subtract, waits=[(SV, K.cnt[SV])])
                vq = v_ts(xqi[:, :], buf[:, :], 127.0, -128.0, ALU.min,
                          ALU.max, waits=[(SV, K.cnt[SV]), (SP, K.cnt[SP])])
                xcons_v[sl] = K.cnt[SV]
                tw = [(SV, vq)]
                if tr_ev:
                    tw.append((SV, tr_ev))
                for b in range(HK):
                    t_tr(ptr[:, b * 128:(b + 1) * 128],
                         xqi[:, b * 128:(b + 1) * 128],
                         waits=(tw if b == 0 else []),
                         inc=(SP if b == HK - 1 else None))
                tr_ev = v_copy(actT_v[:, :, t * 128:(t + 1) * 128],
                               ptr[:, :].rearrange("p (k c) -> p k c", k=HK),
                               waits=[(SP, K.cnt[SP])])
            actT_done = tr_ev

            silu_cnt = {}
            upev_cnt = {}
            h_cnt = {}
            hdma_cnt = {}
            wg_cons = [0, 0]
            for m in range(IT):
                sl = m % 2
                lw = [(SSP, K.cnt[SSP])]
                if wg_cons[sl]:
                    lw.append((SP, wg_cons[sl]))
                K.dma("sync",
                      wgu[sl][:, 0:H].rearrange("p (k c) -> p k c", k=HK),
                      d_gsp[m * HK * 128:(m + 1) * HK * 128, :]
                      .rearrange("(k h) c -> h k c", k=HK), SWG[sl], waits=lw)
                ldu = K.dma("sync",
                            wgu[sl][:, H:2 * H]
                            .rearrange("p (k c) -> p k c", k=HK),
                            d_usp[m * HK * 128:(m + 1) * HK * 128, :]
                            .rearrange("(k h) c -> h k c", k=HK), SWG[sl])
                for tc in range(NTC):
                    gi = m * NTC + tc
                    hg = (gi % 2) * 512
                    pg, pu = pA[:, hg:hg + 512], pB[:, hg:hg + 512]
                    tw = [(SWG[sl], ldu), (SV, actT_done)]
                    if gi >= 2:
                        tw.append((SA, silu_cnt[gi - 2]))
                    for k in range(HK):
                        t_mm(pg, wgu[sl][:, k * 128:(k + 1) * 128],
                             actT_v[:, k, tc * 512:(tc + 1) * 512],
                             k == 0, k == HK - 1,
                             waits=(tw if k == 0 else []))
                    gmm = K.cnt[SP]
                    tw2 = [(SV, upev_cnt[gi - 2])] if gi >= 2 else []
                    for k in range(HK):
                        t_mm(pu,
                             wgu[sl][:, H + k * 128:H + (k + 1) * 128],
                             actT_v[:, k, tc * 512:(tc + 1) * 512],
                             k == 0, k == HK - 1,
                             waits=(tw2 if k == 0 else []))
                    umm = K.cnt[SP]
                    sw_s = [(SP, gmm)]
                    if gi >= 2:
                        sw_s.append((SV, h_cnt[gi - 2]))
                    silu_cnt[gi] = a_act(gs[gi % 2][:, :], pg,
                                         AF.Silu, scale=swxg[:, m:m + 1],
                                         waits=sw_s)
                    uw = [(SP, umm)]
                    if gi >= 2:
                        uw.append((SV, h_cnt[gi - 2]))
                    upev_cnt[gi] = v_ts(up[gi % 2][:, :], pu,
                                        swxu[:, m:m + 1], None, ALU.mult,
                                        waits=uw)
                    hw = [(SA, silu_cnt[gi]), (SV, upev_cnt[gi])]
                    if gi >= 4:
                        hw.append((SHWs[gi % 4], hdma_cnt[gi - 4]))
                    h_cnt[gi] = v_tt(ht[gi % 2][:, :], gs[gi % 2][:, :],
                                     up[gi % 2][:, :], ALU.mult, waits=hw)
                    v_red(hmax_a[:, gi:gi + 1], ht[gi % 2][:, :], ALU.max,
                          absv=True, waits=[(SV, h_cnt[gi])])
                    v_copy(htb[gi % 4][:, :], ht[gi % 2][:, :])
                    hdma_cnt[gi] = K.dma(
                        "gpsimd",
                        d_h[m * 128:(m + 1) * 128, tc * 512:(tc + 1) * 512],
                        htb[gi % 4][:, :], SHWs[gi % 4],
                        waits=[(SV, K.cnt[SV])])
                wg_cons[sl] = K.cnt[SP]

            v_red(colmax[:], hmax_a[:, :], ALU.max, waits=[(SV, K.cnt[SV])])
            dT = K.dma("gpsimd", rowmax[0:1, :], colmax[:, 0:1], STD,
                       waits=[(SV, K.cnt[SV])], gate=True)
            v_red(sc[0:1, 0:1], rowmax[0:1, :], ALU.max, waits=[(STD, dT)])
            d2 = allreduce_max(sc[0:1, 0:1], call * 2 + 1, [(SV, K.cnt[SV])])
            sigmoid_via_exp(sc[0:1, 5:6], clip_s[0:1, 1:2])
            v_tt(sc[0:1, 6:7], sc[0:1, 5:6], gmaxr[0:1, 0:1], ALU.mult,
                 waits=[(STD, d2), (SV, K.cnt[SV])])
            v_recip(sc[0:1, 7:8], sc[0:1, 6:7], waits=[(SV, K.cnt[SV])])
            v_ts(sc[0:1, 7:8], sc[0:1, 7:8], 127.0, None, ALU.mult,
                 waits=[(SV, K.cnt[SV])])
            v_ts(sc[0:1, 8:9], sc[0:1, 6:7], INV127 * 0.5, None, ALU.mult,
                 waits=[(SV, K.cnt[SV])])
            bcast128(bc3[:], sc[0:1, 7:8], waits=[(SV, K.cnt[SV])])
            bcast128(bc4[:], sc[0:1, 8:9])
            hsc_a = K.cnt[SA]

            hf_cons = {}
            sv_at_F = K.cnt[SV]
            hq_cons = [K.cnt[SP], K.cnt[SP]]
            evac_cnt = {}
            ldx = {}
            quant_done = {}
            hfv = tuple(t[:, :] for t in hfb) + tuple(t[:, :] for t in htb)
            hfw_rd = [0]
            SHF8 = SHF4 + SHWF

            def emit_quant_chunk(tc, m0, m1):
                hsl = tc % 2
                for m in range(m0, m1):
                    fsl = m % 8
                    lw = [(s, K.cnt[s]) for s in
                          ("SHWa", "SHWb", "SHWc", "SHWd")]
                    if m >= 8:
                        prev = (tc, m - 8)
                    else:
                        lastm = max(mm2 for mm2 in range(IT)
                                    if mm2 % 8 == m % 8)
                        prev = (tc - 1, lastm)
                    if prev in hf_cons:
                        lw.append((SV, hf_cons[prev]))
                    else:
                        lw.append((SV, sv_at_F))
                    ld = K.dma("sync" if fsl % 2 == 0 else "scalar",
                               hfv[fsl],
                               d_h[m * 128:(m + 1) * 128,
                                   tc * 512:(tc + 1) * 512],
                               SHF8[fsl], waits=lw)
                    qw = [(SHF8[fsl], ld), (SA, hsc_a)]
                    if hq_cons[hsl]:
                        qw.append((SP, hq_cons[hsl]))
                    if hfw_rd[0]:
                        qw.append((SV, hfw_rd[0]))
                    hf_cons[(tc, m)] = v_ts(hfw[:, :], hfv[fsl], bc3[:],
                                            MAGIC, ALU.mult, ALU.add,
                                            waits=qw)
                    v_ts(hfw[:, :], hfw[:, :], MAGIC, 127.0,
                         ALU.subtract, ALU.min, waits=[(SV, K.cnt[SV])])
                    hfw_rd[0] = v_ts(hqs[hsl][:, m * 512:(m + 1) * 512],
                                     hfw[:, :], -128.0, None, ALU.max,
                                     waits=[(SV, K.cnt[SV])])

            def emit_group_tt(tc, tt):
                hsl = tc % 2
                if True:
                    g2 = tc * 4 + tt
                    xsl = g2 % 2
                    xw = [(SV, xcons_v[xsl])]
                    if xcons_a[xsl]:
                        xw.append((SA, xcons_a[xsl]))
                        xcons_a[xsl] = 0
                    ldx[g2] = K.dma("gpsimd", xs[xsl][:, :],
                                    src[g2 * 128:(g2 + 1) * 128, :],
                                    SXG[xsl], waits=xw)
                    for jc in range(2):
                        gidx = tc * 8 + tt * 2 + jc
                        bk = gidx % 2
                        tw = [(SV, quant_done[tc]), (SWL, wdq_ld)]
                        if gidx >= 2:
                            tw.append((SV, evac_cnt[gidx - 2]))
                        for m in range(IT):
                            t_mm(pCD[bk][:, :],
                                 hqs[hsl][:, m * 512 + tt * 128:
                                          m * 512 + (tt + 1) * 128],
                                 wdq[:, m * H + jc * 512:
                                     m * H + (jc + 1) * 512],
                                 m == 0, m == IT - 1,
                                 waits=(tw if m == 0 else []))
                        ow = [(SP, K.cnt[SP]), (SXG[xsl], ldx[g2]),
                              (SA, hsc_a)]
                        if ocons[g2 % 2]:
                            ow.append((SO[g2 % 2], ocons[g2 % 2]))
                        evac_cnt[gidx] = v_stt(
                            xn[g2 % 2][:, jc * 512:(jc + 1) * 512],
                            pCD[bk][:, :], bc4[:],
                            xs[xsl][:, jc * 512:(jc + 1) * 512],
                            ALU.mult, ALU.add, waits=ow)
                    xcons_v[xsl] = K.cnt[SV]
                    ocons[g2 % 2] = K.dma(
                        "gpsimd", dst[g2 * 128:(g2 + 1) * 128, :],
                        xn[g2 % 2][:, :], SO[g2 % 2],
                        waits=[(SV, K.cnt[SV])])

            CH = [(0, 6), (6, 12), (12, 17), (17, 22)]
            for tc in range(NTC + 1):
                for tt in range(4):
                    if tc >= 1:
                        emit_group_tt(tc - 1, tt)
                    if tc < NTC:
                        emit_quant_chunk(tc, CH[tt][0], CH[tt][1])
                if tc < NTC:
                    quant_done[tc] = K.cnt[SV]
                if tc >= 1:
                    hq_cons[(tc - 1) % 2] = K.cnt[SP]
            K.join()
            ffn_st.close()

        ffn(0, d_x, d_xres, skip_A=True, pre_d2=pre_d2_0)

        at_st = contextlib.ExitStack()
        QKT = sb("QKT", [128, 16 * S], BF16, at_st)
        Vpad = sb("Vpad", [128, NTB * NHEADS * 65], BF16, at_st)
        yet = sb("yet", [128, 16384], BF16, at_st)
        ET = (yet[:, 0:8192], yet[:, 8192:16384])
        attnT = sb("attnT", [128, HK * S], BF16, at_st)
        woutb = sb("woutb", [128, HK * 512], BF16, at_st)
        qkw = (sb("qkw0", [128, H], BF16, at_st),
               sb("qkw1", [128, H], BF16, at_st))
        vw0 = sb("vw0", [128, HK * 256], BF16, at_st)
        bc_s = sb("bc_s", [64, 512], BF16, at_st)
        bc_sb = sb("bc_sb", [64, 512], BF16, at_st)
        bc_s2 = (bc_s[:, :], bc_sb[:, :])
        araw = (xn[1][0:65, 0:512], xn[1][0:65, 512:1024])
        recbB0 = sb("recbB0", [1, 512], BF16, at_st)
        recbB1 = sb("recbB1", [1, 512], BF16, at_st)
        recbs = (recbB0[0:1, :], recbB1[0:1, :])
        ones_bf = sb("ones_bf", [1, 128], BF16, at_st)
        ybuf = qkw[0]
        sqb = (sb("sqb0", [128, 512], BF16, at_st),
               sb("sqb1", [128, 512], BF16, at_st))
        cur["sqA"] = xn[0]
        a2_i = [0]
        oc_b0 = []

        a2_ld = {}

        def a2_load(t):
            sl = t % 2
            waits = list(oc_b0 if t < 2 else [])
            if xcons_v[sl]:
                waits.append((SV, xcons_v[sl]))
            if xcons_a[sl]:
                waits.append((SA, xcons_a[sl]))
            a2_ld[t] = K.dma("sync", xs[sl][:, :],
                             d_xres[t * 128:(t + 1) * 128, :],
                             SX[sl], waits=waits)

        def a2_piece(t, phase, piece):
            sl = t % 2
            buf = xs[sl]
            if phase == 0:
                alu = (ALU.add, ALU.max, ALU.min)[piece]
                dst = (S_a, Mx_a, Mn_a)[piece]
                v_red(dst[:, t:t + 1], buf[:, :], alu,
                      waits=[(SX[sl], a2_ld[t])])
            elif piece == 0:
                v_tt(cur["sqA"][:, :], buf[:, :], buf[:, :], ALU.mult,
                     waits=[(SX[sl], a2_ld[t]), (SV, K.cnt[SV])])
            elif piece == 1:
                v_red(Q_a[:, t:t + 1], cur["sqA"][:, :], ALU.add,
                      waits=[(SV, K.cnt[SV])])
                v_ts(m_a[:, t:t + 1], S_a[:, t:t + 1], INV1024, None,
                     ALU.mult, waits=[(SV, K.cnt[SV])])
                v_ts(tmp_a[:, t:t + 1], Q_a[:, t:t + 1], INV1024, None,
                     ALU.mult, waits=[(SV, K.cnt[SV])])
                v_tt(var_a[:, t:t + 1], m_a[:, t:t + 1], m_a[:, t:t + 1],
                     ALU.mult, waits=[(SV, K.cnt[SV])])
                v_tt(var_a[:, t:t + 1], tmp_a[:, t:t + 1], var_a[:, t:t + 1],
                     ALU.subtract, waits=[(SV, K.cnt[SV])])
            else:
                a_act(var_a[:, t:t + 1], var_a[:, t:t + 1], AF.Ln,
                      bias=eps_c[:], waits=[(SV, K.cnt[SV])])
                a_act(r_a[:, t:t + 1], var_a[:, t:t + 1], AF.Exp, scale=-0.5,
                      waits=[(SA, K.cnt[SA])])
                xcons_v[sl] = K.cnt[SV]
                xcons_a[sl] = K.cnt[SA]
                if t + 2 < NTB:
                    a2_load(t + 2)
        Vpad_v = Vpad[:, :].rearrange("p (tt h c) -> p tt h c",
                                      h=NHEADS, c=65)

        vones = K.op("gpsimd",
                     lambda e: e.memset(Vpad_v[:, :, :, 64:65], 1.0), inc=SG)
        vones = K.op("gpsimd", lambda e: e.memset(ones_bf[:], 1.0), inc=SG)
        for eng in ("scalar", "vector"):
            if ocons[0]:
                K.wait(eng, [(SO[0], ocons[0])])
            if ocons[1]:
                K.wait(eng, [(SO[1], ocons[1])])

        for b in range(B_CORE):
            tb = b * NTB
            YT_v = yet[:, :].rearrange("p (k tt) -> p k tt", k=HK)
            tr_ev = 0
            for t in range(NTB):
                g2 = tb + t
                sl = t % 2
                buf = xs[sl]
                waits = [(SV, xcons_v[sl])]
                if xcons_a[sl]:
                    waits.append((SA, xcons_a[sl]))
                ld = K.dma("sync", buf[:, :],
                           d_xres[g2 * 128:(g2 + 1) * 128, :], SX[sl],
                           waits=waits)
                ln_stats_tile(g2, buf, SX[sl], ld, False)
                v_stt(tmp_a[:, g2:g2 + 1], m_a[:, g2:g2 + 1], -1.0,
                      r_a[:, g2:g2 + 1], ALU.mult, ALU.mult,
                      waits=[(SA, K.cnt[SA])])
                ya = a_act(ybuf[:, :], buf[:, :], AF.Identity,
                           bias=tmp_a[:, g2:g2 + 1],
                           scale=r_a[:, g2:g2 + 1],
                           waits=[(SV, K.cnt[SV]), (SP, K.cnt[SP])])
                xcons_v[sl] = K.cnt[SV]
                xcons_a[sl] = ya
                tw = [(SA, ya)]
                if tr_ev:
                    tw.append((SV, tr_ev))
                for bb in range(HK):
                    t_tr(ptr[:, bb * 128:(bb + 1) * 128],
                         ybuf[:, bb * 128:(bb + 1) * 128],
                         waits=(tw if bb == 0 else []),
                         inc=(SP if bb == HK - 1 else None))
                tr_ev = v_copy(YT_v[:, :, t * 128:(t + 1) * 128],
                               ptr[:, :].rearrange("p (k c) -> p k c", k=HK),
                               waits=[(SP, K.cnt[SP])])
            yt_done = tr_ev

            qkv_ev = {}
            qk_cons = [0, 0]
            sq_prev = {}
            red_prev = {}
            svb_start = K.cnt[SV]

            def emit_ssq(pt):
                for qc in range(NQC):
                    ii = pt * NQC + qc
                    ssl = ii % 2
                    sw = []
                    if (ii - 2) in sq_prev:
                        sw.append((SP, sq_prev[ii - 2]))
                    aq = a_act(sqb[ssl][:, :],
                               QKT[:, pt * S + qc * 512:
                                   pt * S + (qc + 1) * 512],
                               AF.Square, waits=sw)
                    mw = [(SA, aq)]
                    if (ii - 1) in red_prev:
                        mw.append((SV, red_prev[ii - 1]))
                    else:
                        mw.append((SV, svb_start))
                        mw.append((SG, K.cnt[SG]))
                    t_mm(pE[0:2, :], ones2[:, :], sqb[ssl][:, :], True,
                         True, waits=mw)
                    sq_prev[ii] = K.cnt[SP]
                    red_prev[ii] = v_red(
                        ssqm[:, ii:ii + 1], pE[0:2, :], ALU.max,
                        waits=[(SP, sq_prev[ii])])

            for c in range(16):
                sl = c % 2
                lw = [(SSP, K.cnt[SSP])]
                if c < 2:
                    lw.append((SP, K.cnt[SP]))
                    lw.append((SA, K.cnt[SA]))
                if qk_cons[sl]:
                    lw.append((SP, qk_cons[sl]))
                ld = K.dma("sync",
                           qkw[sl][:, :].rearrange("p (k c) -> p k c", k=HK),
                           d_qksp[c * HK * 128:(c + 1) * HK * 128, :]
                           .rearrange("(k h) w -> h k w", k=HK),
                           SQK[sl], waits=lw)
                for qc in range(NQC):
                    ii = c * NQC + qc
                    bk = ii % 2
                    tw = [(SQK[sl], ld), (SV, yt_done)]
                    if ii >= 2:
                        tw.append((SA, qkv_ev[ii - 2]))
                    for k in range(HK):
                        t_mm(pCD[bk][:, :], qkw[sl][:, k * 128:(k + 1) * 128],
                             YT_v[:, k, qc * 512:(qc + 1) * 512],
                             k == 0, k == HK - 1,
                             waits=(tw if k == 0 else []))
                    qkv_ev[ii] = a_copy(
                        QKT[:, c * S + qc * 512:c * S + (qc + 1) * 512],
                        pCD[bk][:, :], waits=[(SP, K.cnt[SP])])
                qk_cons[sl] = K.cnt[SP]
                if c >= 2:
                    emit_ssq(c - 2)
            emit_ssq(14)
            emit_ssq(15)
            v_ev = {}
            vsp_v = d_vsp.rearrange("(jc k h) c -> jc h k c", jc=2, k=HK)
            for jc2 in range(4):
                jc, hlf = jc2 // 2, jc2 % 2
                lw = [(SSP, K.cnt[SSP]), (SP, K.cnt[SP])]
                ldv = K.dma("sync",
                            vw0[:, :].rearrange("p (k c) -> p k c", k=HK),
                            vsp_v[jc][:, :, hlf * 256:(hlf + 1) * 256],
                            SVW, waits=lw)
                for tt in range(NTB):
                    bk = tt % 2
                    tw = [(SVW, ldv), (SV, yt_done)]
                    if (jc2, tt - 2) in v_ev:
                        tw.append((SV, v_ev[(jc2, tt - 2)]))
                    elif jc2 > 0:
                        tw.append((SV, v_ev[(jc2 - 1, NTB - 2 + tt)]))
                    elif tt < 2 and (16 * NQC - 2 + tt) in qkv_ev:
                        tw.append((SA, qkv_ev[16 * NQC - 2 + tt]))
                    for k in range(HK):
                        t_mm(pCD[bk][:, 0:256],
                             YT_v[:, k, tt * 128:(tt + 1) * 128],
                             vw0[:, k * 256:(k + 1) * 256],
                             k == 0, k == HK - 1,
                             waits=(tw if k == 0 else []))
                    v_ev[(jc2, tt)] = v_copy(
                        Vpad_v[:, tt, jc2 * 4:(jc2 + 1) * 4, 0:64],
                        pCD[bk][:, 0:256].rearrange("p (h c) -> p h c", c=64),
                        waits=[(SP, K.cnt[SP])])
            v_done = K.cnt[SV]

            red_c = red_prev[16 * NQC - 1]
            v_red(ssqr[:, 0:8],
                  ssqm[0:2, 0:32].rearrange("p (pt qc) -> p pt qc", qc=NQC),
                  ALU.max, waits=[(SV, red_c)])
            v_red(ssqr[:, 8:16],
                  ssqm[0:2, 32:64].rearrange("p (pt qc) -> p pt qc", qc=NQC),
                  ALU.max, waits=[(SV, K.cnt[SV])])
            a_act(ssqr[:, 0:16], ssqr[:, 0:16], AF.Ln,
                  waits=[(SV, K.cnt[SV])])
            a_act(ssqr[:, 0:16], ssqr[:, 0:16], AF.Exp, scale=0.5,
                  waits=[(SA, K.cnt[SA])])
            v_tt(ssqm[0:2, 0:8], ssqr[:, 0:8], ssqr[:, 8:16], ALU.mult,
                 waits=[(SA, K.cnt[SA])])
            v_ts(ssqm[0:2, 0:8], ssqm[0:2, 0:8], -0.125, None, ALU.mult,
                 waits=[(SV, K.cnt[SV])])
            K.dma("gpsimd", negBr[0:1, 0:8], ssqm[0:1, 0:8], STD,
                  waits=[(SV, K.cnt[SV])], gate=True)
            dn2 = K.dma("gpsimd", negBr[0:1, 8:16], ssqm[1:2, 0:8], STD,
                        gate=True)
            t_mm(pE[:, 0:16], ones_row[0:1, 0:128], negBr[0:1, 0:16],
                 True, True,
                 waits=[(STD, dn2), (SA, K.cnt[SA]), (SV, K.cnt[SV])])
            a_copy(negB[:, :].rearrange("p (c2 r) -> p r c2", r=2),
                   pE[:, 0:16].rearrange("p (r c2) -> p r c2", r=2),
                   waits=[(SP, K.cnt[SP])])
            negB_done = K.cnt[SA]
            if b == 1:
                a2_load(0)
                a2_load(1)

            exp_cnt = {}
            pv_mm = {}
            pv_done = {}
            nrm_done = {}
            ep1_cnt = {}
            rec_cnt = {}
            rec2_cnt = {}
            last_rec_rd = [0, 0]
            last_bc_rd = [0, 0]
            last_pe_rd = [0]
            NJ2 = (NHEADS // 2) * NQC

            def heads_of(j2):
                hp, qc = j2 // NQC, j2 % NQC
                return 2 * hp, 2 * hp + 1, hp, qc

            def ep1(j2):
                for hs in range(2):
                    ep1_cnt[(j2, hs)] = v_copy(
                        araw[hs], pCD[hs][0:65, :],
                        waits=[(SP, pv_done[(j2, hs)]), (SV, K.cnt[SV])])
                    rec_cnt[(j2, hs)] = ep1_cnt[(j2, hs)]
                for hs in range(2):
                    rw = [(SV, K.cnt[SV])]
                    if last_rec_rd[hs]:
                        rw.append((SP, last_rec_rd[hs]))
                    v_recip(recb[0:1, :], araw[hs][64:65, :], waits=rw)
                    cw = [(SV, K.cnt[SV])]
                    if last_rec_rd[hs]:
                        cw.append((SP, last_rec_rd[hs]))
                    rec2_cnt[(j2, hs)] = v_copy(recbs[hs], recb[0:1, :],
                                                waits=cw)

            def ep2(j2):
                hA, hB, hp, qc = heads_of(j2)
                for hs in range(2):
                    po = hs * 64
                    bw = [(SV, rec2_cnt[(j2, hs)]), (SA, negB_done)]
                    if last_bc_rd[hs]:
                        bw.append((SV, last_bc_rd[hs]))
                    if last_pe_rd[0]:
                        bw.append((SV, last_pe_rd[0]))
                    t_mm(pE[0:64, :], ones_bf[0:1, 0:64],
                         recbs[hs], True, True, waits=bw)
                    last_rec_rd[hs] = K.cnt[SP]
                    bcp = v_copy(bc_s2[hs], pE[0:64, :],
                                 waits=[(SP, K.cnt[SP])])
                    last_pe_rd[0] = bcp
                    nrm_done[(j2, hs)] = v_tt(
                        attnT[po:po + 64, hp * S + qc * 512:
                              hp * S + (qc + 1) * 512],
                        araw[hs][0:64, :], bc_s2[hs], ALU.mult,
                        waits=[(SV, bcp)])
                    last_bc_rd[hs] = nrm_done[(j2, hs)]

            for x in range(NJ2 + 1):
                curi = x if x < NJ2 else None
                prv = x - 1 if x >= 1 else None
                if curi is not None:
                    chA, chB, chp, cqc = heads_of(curi)
                if prv is not None:
                    phA, phB, php, pqc = heads_of(prv)
                for kt in range(NTB):
                    if prv is not None:
                        for hs, hh in ((0, phA), (1, phB)):
                            tw = ([(SA, exp_cnt[(prv, 1, kt | 1)])]
                                  if hs == 0 else [])
                            if kt == 0:
                                tw += [(SV, v_done), (SG, vones)]
                                if (prv - 1, hs) in rec_cnt:
                                    tw.append((SV, rec_cnt[(prv - 1, hs)]))
                            t_mm(pCD[hs][0:65, :], Vpad_v[:, kt, hh, :],
                                 ET[hs][:, kt * 512:(kt + 1) * 512],
                                 kt == 0, kt == NTB - 1, waits=tw)
                            pv_mm[(prv, hs, kt)] = K.cnt[SP]
                            if kt == NTB - 1:
                                pv_done[(prv, hs)] = K.cnt[SP]
                    if curi is not None:
                        hb = (kt % 2) * 512
                        tw = [(SA, negB_done)]
                        if kt >= 2:
                            tw.append((SA, exp_cnt[(curi, 1,
                                                    (kt - 2) | 1)]))
                        elif curi >= 1:
                            tw.append((SA, exp_cnt[(curi - 1, 1,
                                                    (NTB - 2 + kt) | 1)]))
                        mm_c = {}
                        for hs, pt2, po in ((0, pA, 0), (1, pB, 64)):
                            t_mm(pt2[:, hb:hb + 512],
                                 QKT[po:po + 64,
                                     (8 + chp) * S + kt * 128:
                                     (8 + chp) * S + (kt + 1) * 128],
                                 QKT[po:po + 64,
                                     chp * S + cqc * 512:
                                     chp * S + (cqc + 1) * 512],
                                 True, True, waits=(tw if hs == 0 else []))
                            mm_c[hs] = K.cnt[SP]
                        if kt % 2 == 1:
                            for hs, pt2, hh in ((0, pA, chA), (1, pB, chB)):
                                ew = [(SP, mm_c[hs])]
                                if prv is not None:
                                    ew.append((SP, pv_mm[(prv, hs, kt)]))
                                exp_cnt[(curi, hs, kt)] = a_act(
                                    ET[hs][:, (kt - 1) * 512:
                                            (kt + 1) * 512],
                                    pt2[:, 0:1024], AF.Exp,
                                    bias=negB[:, hh:hh + 1],
                                    scale=0.125, waits=ew)
                    if kt == 6 and x >= 2:
                        ep2(x - 2)
                    if kt in (6, 10, 14) and b == 1 and x < 2 * NTB:
                        a2_piece(x // 2, x % 2, {6: 0, 10: 1, 14: 2}[kt])
                if prv is not None:
                    ep1(prv)
            ep2(NJ2 - 1)
            att_done = K.cnt[SV]
            op_ev = {}
            for jc in range(2):
                for k in range(HK):
                    sl = k % 2
                    w = [(SV, xcons_v[sl])]
                    if xcons_a[sl]:
                        w.append((SA, xcons_a[sl]))
                        xcons_a[sl] = 0
                    if jc == 0 and k < 2:
                        w.append((SV, att_done))
                    ldw = K.dma("sync", xs[sl][:, 0:512],
                                d_wout[k * 128:(k + 1) * 128,
                                       jc * 512:(jc + 1) * 512],
                                SX[sl], waits=w)
                    xcons_v[sl] = v_copy(woutb[:, k * 512:(k + 1) * 512],
                                         xs[sl][:, 0:512],
                                         waits=[(SX[sl], ldw)] +
                                         ([(SP, K.cnt[SP])] if jc else []))
                wout_done = K.cnt[SV]
                for qt in range(NTB):
                    g2 = tb + qt
                    xsl = qt % 2
                    xw = [(SV, xcons_v[xsl])]
                    ldx = K.dma("sync", xs[xsl][:, :],
                                d_xres[g2 * 128:(g2 + 1) * 128, :], SX[xsl],
                                waits=xw)
                    gq = jc * NTB + qt
                    bk = gq % 2
                    tw = [(SV, att_done), (SV, wout_done)]
                    if gq >= 2:
                        tw.append((SV, op_ev[gq - 2]))
                    for k in range(HK):
                        t_mm(pCD[bk][:, :],
                             attnT[:, k * S + qt * 128:k * S + (qt + 1) * 128],
                             woutb[:, k * 512:(k + 1) * 512],
                             k == 0, k == HK - 1,
                             waits=(tw if k == 0 else []))
                    ow = [(SP, K.cnt[SP]), (SX[xsl], ldx)]
                    if ocons[xsl]:
                        ow.append((SO[xsl], ocons[xsl]))
                    op_ev[gq] = v_tt(
                        xn[xsl][:, jc * 512:(jc + 1) * 512], pCD[bk][:, :],
                        xs[xsl][:, jc * 512:(jc + 1) * 512], ALU.add,
                        waits=ow)
                    xcons_v[xsl] = K.cnt[SV]
                    ocons[xsl] = K.dma(
                        "gpsimd",
                        d_xres[g2 * 128:(g2 + 1) * 128,
                               jc * 512:(jc + 1) * 512],
                        xn[xsl][:, jc * 512:(jc + 1) * 512], SO[xsl],
                        waits=[(SV, K.cnt[SV])])
            if b == 0:
                oc_b0 = [(SO[i], ocons[i]) for i in range(2) if ocons[i]]
            K.join()

        oc_b1 = [(SO[i], ocons[i]) for i in range(2) if ocons[i]]
        for t in range(NTB, NT):
            ffn_stage_A(d_xres, t, extraw=(oc_b1 if t == NTB else ()))
        pre_d2_1 = stats_tail(2)
        at_st.close()

        ffn(1, d_xres, d_o, skip_A=True, pre_d2=pre_d2_1)

        K.wait("gpsimd", [(SO[0], K.cnt["SO0"]), (SO[1], K.cnt["SO1"])])

        blk = st.enter_context(nc.Block())
        K.realize(blk)
    return nc


def kernel(L0_attn_X, L0_attn_W_qkv, L0_attn_W_out, L0_ffn_clip_x,
           L0_ffn_clip_down, L0_ffn_w_gate, L0_ffn_w_up, L0_ffn_w_down):
    X = np.ascontiguousarray(np.asarray(L0_attn_X, dtype=np.float32))
    wqkv = np.asarray(L0_attn_W_qkv, dtype=np.float32)
    wqkv_p = np.ascontiguousarray(
        wqkv.reshape(H, NHEADS, 64, 3).transpose(0, 3, 1, 2)
        .reshape(H, 3 * H))
    wout = np.ascontiguousarray(np.asarray(L0_attn_W_out, dtype=np.float32))
    wg = np.ascontiguousarray(np.asarray(L0_ffn_w_gate, dtype=np.float32))
    wu = np.ascontiguousarray(np.asarray(L0_ffn_w_up, dtype=np.float32))
    wd = np.ascontiguousarray(np.asarray(L0_ffn_w_down, dtype=np.float32))
    cx = np.ascontiguousarray(
        np.asarray(L0_ffn_clip_x, dtype=np.float32).reshape(1, 1))
    cd = np.ascontiguousarray(
        np.asarray(L0_ffn_clip_down, dtype=np.float32).reshape(1, 1))

    nc = build_program()
    in_maps = []
    for c in range(N_CORES):
        xsh = np.ascontiguousarray(
            X[c * B_CORE:(c + 1) * B_CORE].reshape(T, H))
        in_maps.append(dict(x=xsh, wqkv=wqkv_p, wout=wout, wg=wg, wu=wu,
                            wd=wd, clipx=cx, clipd=cd))
    res = run_bass_kernel_spmd(nc, in_maps, core_ids=list(range(N_CORES)))
    out = np.concatenate(
        [r["o"].reshape(B_CORE, S, H) for r in res.results], axis=0)
    return out


# version 26
# speedup vs baseline: 1.1564x; 1.1564x over previous
import contextlib
import numpy as np
import ml_dtypes
import concourse.bass as bass
import concourse.mybir as mybir
from concourse.bass_utils import run_bass_kernel_spmd

F32 = mybir.dt.float32
BF16 = mybir.dt.bfloat16
AF = mybir.ActivationFunctionType
ALU = mybir.AluOpType
AX = mybir.AxisListType

MAGIC = 1.5 * 2.0 ** 23
EPS = 1e-5
INV127 = 1.0 / 127.0
INV1024 = 1.0 / 1024.0

ENGS = ("sync", "vector", "scalar", "tensor", "gpsimd")

B_CORE = 2
S = 2048
T = B_CORE * S
H = 1024
HK = H // 128
NT = T // 128
NTB = S // 128
I_DIM = 2816
IT = I_DIM // 128
NTC = T // 512
NQC = S // 512
NHEADS = 16
N_CORES = 8


class Sched:
    def __init__(self, nc, st):
        self.nc = nc
        self.st = st
        self.ops = {e: [] for e in ENGS}
        self.sems = {}
        self.cnt = {}
        self.hw = {e: {} for e in ENGS}

    def sem(self, name):
        if name not in self.sems:
            self.sems[name] = self.st.enter_context(self.nc.semaphore(name))
            self.cnt[name] = 0
        return name

    def op(self, eng, fn, waits=(), inc=None, n=1):
        pruned = []
        for s, v in waits:
            if v > self.hw[eng].get(s, 0):
                pruned.append((s, v))
                self.hw[eng][s] = v
        self.ops[eng].append((fn, pruned, inc, n))
        if inc is not None:
            self.cnt[inc] += n
            return self.cnt[inc]
        return None

    def dma(self, eng, out, in_, sem, waits=(), gate=False):
        if gate and self.cnt.get(sem, 0):
            waits = list(waits) + [(sem, self.cnt[sem])]
        return self.op(eng, lambda e, o=out, i=in_: e.dma_start(out=o, in_=i),
                       waits, inc=sem, n=16)

    def wait(self, eng, waits):
        self.op(eng, None, waits)

    def join(self):
        for eng in ENGS:
            for s, c in self.cnt.items():
                if c > 0:
                    self.wait(eng, [(s, c)])

    def realize(self, blk):
        sems = self.sems
        for eng in ENGS:
            ops = self.ops[eng]

            def body(e, ops=ops):
                for fn, waits, inc, n in ops:
                    for s, v in waits:
                        e.wait_ge(sems[s], v)
                    if fn is not None:
                        ins = fn(e)
                        if inc is not None:
                            ins.then_inc(sems[inc], n)
            getattr(blk, eng)(body)


def build_program(name="mkmac"):
    nc = bass.Bass("TRN2", target_bir_lowering=False, name=name)

    d_x = nc.dram_tensor("x", [T, H], F32, kind="ExternalInput").ap()
    d_wqkv = nc.dram_tensor("wqkv", [H, 3 * H], F32, kind="ExternalInput").ap()
    d_wout = nc.dram_tensor("wout", [H, H], F32, kind="ExternalInput").ap()
    d_wg = nc.dram_tensor("wg", [I_DIM, H], F32, kind="ExternalInput").ap()
    d_wu = nc.dram_tensor("wu", [I_DIM, H], F32, kind="ExternalInput").ap()
    d_wd = nc.dram_tensor("wd", [H, I_DIM], F32, kind="ExternalInput").ap()
    d_clipx = nc.dram_tensor("clipx", [1, 1], F32, kind="ExternalInput").ap()
    d_clipd = nc.dram_tensor("clipd", [1, 1], F32, kind="ExternalInput").ap()
    d_o = nc.dram_tensor("o", [T, H], F32, kind="ExternalOutput").ap()

    d_xres = nc.dram_tensor("xres", [T, H], F32, kind="Internal").ap()
    d_h = nc.dram_tensor("hbuf", [I_DIM, T], BF16, kind="Internal").ap()
    d_gsp = nc.dram_tensor("gsp", [I_DIM * HK, 128], BF16, kind="Internal").ap()
    d_usp = nc.dram_tensor("usp", [I_DIM * HK, 128], BF16, kind="Internal").ap()
    d_dsp = nc.dram_tensor("dsp", [I_DIM, H], BF16, kind="Internal").ap()
    d_qksp = nc.dram_tensor("qksp", [16 * HK * 128, 128], BF16,
                            kind="Internal").ap()
    d_vsp = nc.dram_tensor("vsp", [2 * HK * 128, 512], BF16,
                           kind="Internal").ap()
    d_cc = [(nc.dram_tensor(f"ccin{i}", [1, 128], F32, kind="Internal").ap(),
             nc.dram_tensor(f"ccout{i}", [1, 128], F32, kind="Internal").ap())
            for i in range(4)]

    st = contextlib.ExitStack()
    with st:
        K = Sched(nc, st)

        def sb(n, sh, dt, stk=None):
            return (stk or st).enter_context(nc.sbuf_tensor(n, sh, dt))

        ident = sb("ident", [128, 128], BF16)
        ones_row = sb("ones_row", [1, 128], F32)
        ones2 = sb("ones2", [128, 2], BF16)
        swg = sb("swg", [128, IT], F32)
        swu = sb("swu", [128, IT], F32)
        swxg = sb("swxg", [128, IT], F32)
        swxu = sb("swxu", [128, IT], F32)
        S_a = sb("S_a", [128, NT], F32)
        Q_a = sb("Q_a", [128, NT], F32)
        Mx_a = sb("Mx_a", [128, NT], F32)
        Mn_a = sb("Mn_a", [128, NT], F32)
        m_a = sb("m_a", [128, NT], F32)
        r_a = sb("r_a", [128, NT], F32)
        A_a = sb("A_a", [128, NT], F32)
        B_a = sb("B_a", [128, NT], F32)
        var_a = sb("var_a", [128, NT], F32)
        tmp_a = sb("tmp_a", [128, NT], F32)
        hmax_a = sb("hmax_a", [128, IT * NTC], F32)
        colmax = sb("colmax", [128, 1], F32)
        rowmax = sb("rowmax", [1, 128], F32)
        ccrow = sb("ccrow", [1, 128], F32)
        gmaxr = sb("gmaxr", [1, 128], F32)
        sc = sb("sc", [1, 16], F32)
        clip_s = sb("clip_s", [1, 2], F32)
        bc1 = sb("bc1", [128, 1], F32)
        bc2 = sb("bc2", [128, 1], F32)
        bc3 = sb("bc3", [128, 1], F32)
        bc4 = sb("bc4", [128, 1], F32)
        negB = sb("negB", [128, NHEADS], F32)
        xs = (sb("xs0", [128, H], F32), sb("xs1", [128, H], F32))
        xn = (sb("xn0", [128, H], F32), sb("xn1", [128, H], F32))
        recb = sb("recb", [1, 512], F32)
        ssqm = sb("ssqm", [2, 64], F32)
        ssqr = sb("ssqr", [2, 16], F32)
        negBr = sb("negBr", [1, 16], F32)
        eps_c = sb("eps_c", [128, 1], F32)
        nmag_c = sb("nmag_c", [128, 1], F32)
        mag_c = sb("mag_c", [128, 1], F32)

        pA = st.enter_context(nc.psum_tensor("pA", [128, 1024], F32))
        pB = st.enter_context(nc.psum_tensor("pB", [128, 1024], F32))
        pC = st.enter_context(nc.psum_tensor("pC", [128, 512], F32))
        pD = st.enter_context(nc.psum_tensor("pD", [128, 512], F32))
        pE = st.enter_context(nc.psum_tensor("pE", [128, 512], F32))
        ptr = st.enter_context(nc.psum_tensor("ptr", [128, 1024], BF16))
        pCD = (pC, pD)

        SV = K.sem("SV")
        SA = K.sem("SA")
        SP = K.sem("SP")
        SG = K.sem("SG")
        SX = [K.sem("SX0"), K.sem("SX1")]
        SWL = K.sem("SWL")
        SHW = K.sem("SHW")
        SHF4 = [K.sem("SHF0"), K.sem("SHF1"), K.sem("SHF2"), K.sem("SHF3")]
        SHWF = [K.sem("SHF4"), K.sem("SHF5"), K.sem("SHF6"), K.sem("SHF7")]
        SWG = [K.sem("SWG0"), K.sem("SWG1")]
        SPL = [K.sem("SPL0"), K.sem("SPL1")]
        SXG = [K.sem("SXG0"), K.sem("SXG1")]
        SHWs = [K.sem("SHWa"), K.sem("SHWb"), K.sem("SHWc"), K.sem("SHWd")]
        SQK = [K.sem("SQK0"), K.sem("SQK1")]
        SVW = K.sem("SVW")
        SO = [K.sem("SO0"), K.sem("SO1")]
        STD = K.sem("STD")
        SSP = K.sem("SSP")
        SCC = K.sem("SCC")

        cur = {"sqA": None}
        xcons_v = [0, 0]
        xcons_a = [0, 0]
        ocons = [0, 0]

        def v_ts(out, in0, s1, s2, o1, o2=ALU.bypass, waits=(), inc=SV):
            return K.op("vector",
                        lambda e, o=out, i=in0, a=s1, b=s2, x=o1, y=o2:
                        e.tensor_scalar(out=o, in0=i, scalar1=a, scalar2=b,
                                        op0=x, op1=y), waits, inc=inc)

        def v_tt(out, in0, in1, alu, waits=(), inc=SV):
            return K.op("vector",
                        lambda e, o=out, a=in0, b=in1, x=alu:
                        e.tensor_tensor(out=o, in0=a, in1=b, op=x),
                        waits, inc=inc)

        def v_stt(out, in0, scal, in1, o1, o2, waits=(), inc=SV):
            return K.op("vector",
                        lambda e, o=out, a=in0, s=scal, b=in1, x=o1, y=o2:
                        e.scalar_tensor_tensor(out=o, in0=a, scalar=s, in1=b,
                                               op0=x, op1=y), waits, inc=inc)

        def v_red(out, in_, alu, absv=False, waits=(), inc=SV):
            return K.op("vector",
                        lambda e, o=out, i=in_, a=alu, ab=absv:
                        e.tensor_reduce(out=o, in_=i, axis=AX.X, op=a,
                                        apply_absolute_value=ab),
                        waits, inc=inc)

        def v_copy(out, in_, waits=(), inc=SV):
            return K.op("vector",
                        lambda e, o=out, i=in_: e.tensor_copy(out=o, in_=i),
                        waits, inc=inc)

        def v_recip(out, in_, waits=(), inc=SV):
            return K.op("vector",
                        lambda e, o=out, i=in_: e.reciprocal(out=o, in_=i),
                        waits, inc=inc)

        def a_act(out, in_, func, bias=0.0, scale=1.0, accum=None, waits=(),
                  inc=SA):
            return K.op("scalar",
                        lambda e, o=out, i=in_, f=func, b=bias, s=scale,
                        ac=accum: e.activation(out=o, in_=i, func=f, bias=b,
                                               scale=s, accum_out=ac),
                        waits, inc=inc)

        def a_copy(out, in_, waits=(), inc=SA):
            return K.op("scalar",
                        lambda e, o=out, i=in_: e.copy(out=o, in_=i),
                        waits, inc=inc)

        def t_mm(out, lhsT, rhs, start, stop, waits=(), inc="auto"):
            if inc == "auto":
                inc = SP if stop else None
            return K.op("tensor",
                        lambda e, o=out, l=lhsT, r=rhs, s0=start, s1=stop:
                        e.matmul(o, l, r, start=s0, stop=s1), waits, inc=inc)

        def t_tr(out, in_, waits=(), inc=SP):
            return K.op("tensor",
                        lambda e, o=out, i=in_: e.transpose(o, i, ident[:]),
                        waits, inc=inc)

        def bcast128(dst, src11, waits=()):
            t_mm(pE[:, 0:1], ones_row[0:1, 0:128], src11, True, True,
                 waits=list(waits) + [(SA, K.cnt[SA])])
            return a_copy(dst, pE[:, 0:1], waits=[(SP, K.cnt[SP])])

        def sigmoid_via_exp(dst, src11, waits=()):
            a_act(dst, src11, AF.Exp, scale=-1.0, waits=list(waits) +
                  [(SV, K.cnt[SV])])
            v_ts(dst, dst, 1.0, None, ALU.add, waits=[(SA, K.cnt[SA])])
            return v_recip(dst, dst, waits=[(SV, K.cnt[SV])])

        def allreduce_max(src11, idx, waits):
            vb = v_ts(ccrow[0:1, :], src11.broadcast_to((1, 128)), 1.0, None,
                      ALU.mult, waits=list(waits) + [(STD, K.cnt[STD])])
            d1 = K.dma("gpsimd", d_cc[idx][0], ccrow[0:1, :], STD,
                       waits=[(SV, vb)])
            ccv = K.op("gpsimd",
                       lambda e, i=idx: e.collective_compute(
                           "AllReduce", ALU.max,
                           replica_groups=[list(range(N_CORES))],
                           ins=[d_cc[i][0].opt()], outs=[d_cc[i][1].opt()]),
                       waits=[(STD, d1)], inc=SCC, n=1)
            return K.dma("gpsimd", gmaxr[0:1, :], d_cc[idx][1], STD,
                         waits=[(SCC, ccv), (SV, K.cnt[SV])], gate=True)

        def ln_stats_tile(t, buf, ld_sem, ld_cnt, minmax, vsq=False):
            v_red(S_a[:, t:t + 1], buf[:, :], ALU.add,
                  waits=[(ld_sem, ld_cnt)])
            if minmax:
                v_red(Mx_a[:, t:t + 1], buf[:, :], ALU.max)
                v_red(Mn_a[:, t:t + 1], buf[:, :], ALU.min)
            if vsq:
                v_tt(cur["sqA"][:, :], buf[:, :], buf[:, :], ALU.mult,
                     waits=[(ld_sem, ld_cnt), (SV, K.cnt[SV])])
                v_red(Q_a[:, t:t + 1], cur["sqA"][:, :], ALU.add,
                      waits=[(SV, K.cnt[SV])])
            else:
                a_act(cur["sqA"][:, :], buf[:, :], AF.Square,
                      accum=Q_a[:, t:t + 1],
                      waits=[(ld_sem, ld_cnt), (SA, K.cnt[SA])])
            v_ts(m_a[:, t:t + 1], S_a[:, t:t + 1], INV1024, None, ALU.mult,
                 waits=[(SV, K.cnt[SV])])
            v_ts(tmp_a[:, t:t + 1], Q_a[:, t:t + 1], INV1024, None, ALU.mult,
                 waits=[(SA, K.cnt[SA])])
            v_tt(var_a[:, t:t + 1], m_a[:, t:t + 1], m_a[:, t:t + 1],
                 ALU.mult, waits=[(SV, K.cnt[SV])])
            v_tt(var_a[:, t:t + 1], tmp_a[:, t:t + 1], var_a[:, t:t + 1],
                 ALU.subtract, waits=[(SV, K.cnt[SV])])
            a_act(var_a[:, t:t + 1], var_a[:, t:t + 1], AF.Ln,
                  bias=eps_c[:], waits=[(SV, K.cnt[SV])])
            a_act(r_a[:, t:t + 1], var_a[:, t:t + 1], AF.Exp, scale=-0.5,
                  waits=[(SA, K.cnt[SA])])

        def ffn_stage_A(src, t, vsq=False, extraw=()):
            sl = t % 2
            buf = xs[sl]
            waits = list(extraw)
            if xcons_v[sl]:
                waits.append((SV, xcons_v[sl]))
            if xcons_a[sl]:
                waits.append((SA, xcons_a[sl]))
            ld = K.dma("sync", buf[:, :], src[t * 128:(t + 1) * 128, :],
                       SX[sl], waits=waits)
            ln_stats_tile(t, buf, SX[sl], ld, True, vsq=vsq)
            xcons_v[sl] = K.cnt[SV]
            xcons_a[sl] = K.cnt[SA]

        def stats_tail(ar_idx):
            v_tt(Mx_a[:, :], Mx_a[:, :], m_a[:, :], ALU.subtract,
                 waits=[(SV, K.cnt[SV]), (SA, K.cnt[SA])])
            v_tt(Mn_a[:, :], m_a[:, :], Mn_a[:, :], ALU.subtract,
                 waits=[(SV, K.cnt[SV])])
            v_tt(Mx_a[:, :], Mx_a[:, :], Mn_a[:, :], ALU.max,
                 waits=[(SV, K.cnt[SV])])
            v_tt(Mx_a[:, :], Mx_a[:, :], r_a[:, :], ALU.mult,
                 waits=[(SV, K.cnt[SV])])
            v_red(colmax[:], Mx_a[:, :], ALU.max, waits=[(SV, K.cnt[SV])])
            dT = K.dma("gpsimd", rowmax[0:1, :], colmax[:, 0:1], STD,
                       waits=[(SV, K.cnt[SV])], gate=True)
            v_red(sc[0:1, 0:1], rowmax[0:1, :], ALU.max, waits=[(STD, dT)])
            return allreduce_max(sc[0:1, 0:1], ar_idx, [(SV, K.cnt[SV])])


        g = K.op("gpsimd", lambda e: e.memset(ident[:], 0.0), inc=SG)
        g = K.op("gpsimd",
                 lambda e: e.affine_select(out=ident[:], in_=ident[:],
                                           compare_op=ALU.not_equal, fill=1.0,
                                           base=0, pattern=[[-1, 128]],
                                           channel_multiplier=1),
                 waits=[(SG, g)], inc=SG)
        IDENT_SG = g
        K.op("gpsimd", lambda e: e.memset(ones_row[:], 1.0), inc=SG)
        g = K.op("gpsimd", lambda e: e.memset(ones2[:], 0.0), inc=SG)
        g = K.op("gpsimd", lambda e: e.memset(ones2[0:64, 0:1], 1.0),
                 waits=[(SG, g)], inc=SG)
        K.op("gpsimd", lambda e: e.memset(ones2[64:128, 1:2], 1.0),
             waits=[(SG, g)], inc=SG)
        K.op("gpsimd", lambda e: e.memset(eps_c[:], EPS), inc=SG)
        K.op("gpsimd", lambda e: e.memset(mag_c[:], MAGIC), inc=SG)
        CONST_SG = K.op("gpsimd", lambda e: e.memset(nmag_c[:], -MAGIC),
                        inc=SG)
        K.wait("scalar", [(SG, CONST_SG)])
        K.dma("sync", clip_s[0:1, 0:1], d_clipx, SWL)
        K.dma("sync", clip_s[0:1, 1:2], d_clipd, SWL)

        prep_st = contextlib.ExitStack()
        wld = (sb("wld0", [128, 3 * H], F32, prep_st),
               sb("wld1", [128, 3 * H], F32, prep_st))
        wint = sb("wint", [128, 3 * H], BF16, prep_st)
        wtr = sb("wtr", [128, I_DIM], BF16, prep_st)
        wrow = sb("wrow", [128, 1], F32, prep_st)
        wrcp = sb("wrcp", [128, 1], F32, prep_st)

        cons = [0, 0]
        last_ev = [0]
        last_sp = [0]
        slot_i = [0]
        cur["sqA"] = xn[0]
        K.wait("tensor", [(SG, IDENT_SG)])

        def prep_rowtile(src_ap, width, fold, sw_col, spill_fn, idx):
            sl = slot_i[0] % 2
            slot_i[0] += 1
            buf = wld[sl]
            waits = [(SV, cons[sl])] if cons[sl] else []
            ld = K.dma("sync", buf[:, 0:width], src_ap, SPL[sl], waits=waits)
            v_red(wrow[:], buf[:, 0:width], ALU.max, absv=True,
                  waits=[(SPL[sl], ld), (SV, K.cnt[SV])])
            v_ts(wrow[:], wrow[:], INV127, None, ALU.mult,
                 waits=[(SV, K.cnt[SV])])
            v_recip(wrcp[:], wrow[:], waits=[(SV, K.cnt[SV])])
            a_act(buf[:, 0:width], buf[:, 0:width], AF.Identity,
                  bias=mag_c[:], scale=wrcp[:],
                  waits=[(SV, K.cnt[SV]), (SA, K.cnt[SA])])
            wi_w = [(SV, K.cnt[SV]), (SA, K.cnt[SA])]
            if last_sp[0]:
                wi_w.append((SSP, last_sp[0]))
            if fold:
                vq = v_ts(wint[:, 0:width], buf[:, 0:width], MAGIC, wrow[:],
                          ALU.subtract, ALU.mult, waits=wi_w)
            else:
                vq = v_ts(wint[:, 0:width], buf[:, 0:width], MAGIC, None,
                          ALU.subtract, waits=wi_w)
                v_copy(sw_col, wrow[:], waits=[(SV, vq)])
            cons[sl] = K.cnt[SV]
            wt = width // 128
            for c0 in range(0, wt, 8):
                nb = min(8, wt - c0)
                tw = [(SV, vq)]
                if last_ev[0]:
                    tw.append((SA, last_ev[0]))
                for b in range(nb):
                    t_tr(ptr[:, b * 128:(b + 1) * 128],
                         wint[:, (c0 + b) * 128:(c0 + b + 1) * 128],
                         waits=(tw if b == 0 else []),
                         inc=(SP if b == nb - 1 else None))
                last_ev[0] = a_copy(wtr[:, c0 * 128:(c0 + nb) * 128],
                                    ptr[:, 0:nb * 128],
                                    waits=[(SP, K.cnt[SP])])
            last_sp[0] = spill_fn(idx, wt)

        def mk_spill_gu(dst_sp):
            def f(m, wt):
                return K.dma(
                    "gpsimd",
                    dst_sp[m * wt * 128:(m + 1) * wt * 128, :]
                    .rearrange("(k h) c -> h k c", k=wt),
                    wtr[:, 0:wt * 128].rearrange("p (k c) -> p k c", k=wt),
                    SSP, waits=[(SA, last_ev[0])], gate=True)
            return f

        def sp_d(jt, wt):
            return K.dma(
                "gpsimd",
                d_dsp[:, jt * 128:(jt + 1) * 128]
                .rearrange("(k h) c -> h k c", k=IT),
                wtr[:, 0:IT * 128].rearrange("p (k c) -> p k c", k=IT),
                SSP, waits=[(SA, last_ev[0])], gate=True)

        a_tile = [0]

        def emit_A_tiles(n):
            while a_tile[0] < NT and n > 0:
                ffn_stage_A(d_x, a_tile[0])
                a_tile[0] += 1
                n -= 1

        f_g = mk_spill_gu(d_gsp)
        for m in range(IT):
            prep_rowtile(d_wg[m * 128:(m + 1) * 128, :], H, False,
                         swg[:, m:m + 1], f_g, m)
            emit_A_tiles(1)
        f_u = mk_spill_gu(d_usp)
        for m in range(IT):
            prep_rowtile(d_wu[m * 128:(m + 1) * 128, :], H, False,
                         swu[:, m:m + 1], f_u, m)
            emit_A_tiles(1)
        emit_A_tiles(NT)
        pre_d2_0 = stats_tail(0)

        for jt in range(HK):
            prep_rowtile(d_wd[jt * 128:(jt + 1) * 128, :], I_DIM, True,
                         None, sp_d, jt)
        qcons_a = [0, 0]
        for k in range(HK):
            sl = slot_i[0] % 2
            slot_i[0] += 1
            buf = wld[sl]
            if qcons_a[sl]:
                waits = [(SA, qcons_a[sl])]
            else:
                waits = [(SV, cons[sl])] if cons[sl] else []
            ld = K.dma("sync", buf[:, 0:3 * H],
                       d_wqkv[k * 128:(k + 1) * 128, :], SPL[sl], waits=waits)
            wi_w = [(SPL[sl], ld), (SV, K.cnt[SV]), (SA, K.cnt[SA])]
            if last_sp[0]:
                wi_w.append((SSP, last_sp[0]))
            vq = a_copy(wint[:, 0:3 * H], buf[:, 0:3 * H], waits=wi_w)
            qcons_a[sl] = K.cnt[SA]
            K.dma("gpsimd",
                  d_qksp.rearrange("(c k h) w -> k h c w", c=16, k=HK)[k],
                  wint[:, 0:2048].rearrange("h (c w) -> h c w", w=128),
                  SSP, waits=[(SA, vq)], gate=True)
            last_sp[0] = K.dma(
                "gpsimd",
                d_vsp.rearrange("(jc k h) w -> k h jc w", jc=2, k=HK)[k],
                wint[:, 2048:3072].rearrange("h (jc w) -> h jc w", w=512),
                SSP, waits=[(SA, vq)], gate=True)

        emit_A_tiles(NT)
        K.join()
        prep_st.close()

        def ffn(call, src, dst, skip_A=False, pre_d2=None):
            ffn_st = contextlib.ExitStack()
            actT = sb(f"actT{call}", [128, HK * T], BF16, ffn_st)
            wdq = sb(f"wdq{call}", [128, IT * H], BF16, ffn_st)
            hqs = (sb(f"hqs0_{call}", [128, IT * 512], BF16, ffn_st),
                   sb(f"hqs1_{call}", [128, IT * 512], BF16, ffn_st))
            wgu = (sb(f"wgu0_{call}", [128, 2 * H], BF16, ffn_st),
                   sb(f"wgu1_{call}", [128, 2 * H], BF16, ffn_st))
            gs = (sb(f"gs0_{call}", [128, 512], F32, ffn_st),
                  sb(f"gs1_{call}", [128, 512], F32, ffn_st))
            up = (sb(f"up0_{call}", [128, 512], F32, ffn_st),
                  sb(f"up1_{call}", [128, 512], F32, ffn_st))
            ht = tuple(sb(f"ht{i}_{call}", [128, 512], F32, ffn_st)
                       for i in range(2))
            htb = tuple(sb(f"htb{i}_{call}", [128, 512], BF16, ffn_st)
                        for i in range(4))
            hfb = tuple(sb(f"hfb{i}_{call}", [128, 512], BF16, ffn_st)
                        for i in range(4))
            hfw = sb(f"hfw_{call}", [128, 512], F32, ffn_st)
            cur["sqA"] = xn[0]
            if ocons[0]:
                K.wait("scalar", [(SO[0], ocons[0])])
            if ocons[1]:
                K.wait("scalar", [(SO[1], ocons[1])])
            xqi = hqs[0][:, 0:H]
            actT_v = actT[:, :].rearrange("p (k tt) -> p k tt", k=HK)

            wdq_ld = K.dma("sync",
                           wdq[:, :].rearrange("p (m c) -> p m c", m=IT),
                           d_dsp.rearrange("(m p) c -> p m c", m=IT),
                           SWL, waits=[(SSP, K.cnt[SSP])], gate=True)

            if not skip_A:
                for t in range(NT):
                    ffn_stage_A(src, t)
            if pre_d2 is None:
                d2 = stats_tail(call * 2)
            else:
                d2 = pre_d2
            sigmoid_via_exp(sc[0:1, 1:2], clip_s[0:1, 0:1],
                            waits=[(SWL, K.cnt[SWL])])
            v_tt(sc[0:1, 2:3], sc[0:1, 1:2], gmaxr[0:1, 0:1], ALU.mult,
                 waits=[(STD, d2), (SV, K.cnt[SV])])
            v_recip(sc[0:1, 3:4], sc[0:1, 2:3], waits=[(SV, K.cnt[SV])])
            v_ts(sc[0:1, 3:4], sc[0:1, 3:4], 127.0, None, ALU.mult,
                 waits=[(SV, K.cnt[SV])])
            v_ts(sc[0:1, 4:5], sc[0:1, 2:3], INV127, None, ALU.mult,
                 waits=[(SV, K.cnt[SV])])
            bcast128(bc1[:], sc[0:1, 3:4], waits=[(SV, K.cnt[SV])])
            bcast128(bc2[:], sc[0:1, 4:5])
            acd = K.cnt[SA]
            v_ts(A_a[:, :], r_a[:, :], bc1[:], None, ALU.mult,
                 waits=[(SA, acd)])
            v_stt(B_a[:, :], m_a[:, :], -1.0, A_a[:, :], ALU.mult, ALU.mult,
                  waits=[(SV, K.cnt[SV])])
            v_ts(swxg[:, :], swg[:, :], bc2[:], None, ALU.mult,
                 waits=[(SA, acd)])
            v_ts(swxu[:, :], swu[:, :], bc2[:], None, ALU.mult)
            ab_done = K.cnt[SV]

            tr_ev = 0
            for t in range(NT):
                sl = t % 2
                buf = xs[sl]
                ld = K.dma("sync", buf[:, :], src[t * 128:(t + 1) * 128, :],
                           SX[sl], waits=[(SV, xcons_v[sl])])
                v_ts(buf[:, :], buf[:, :], A_a[:, t:t + 1], B_a[:, t:t + 1],
                     ALU.mult, ALU.add, waits=[(SX[sl], ld), (SV, ab_done)])
                v_ts(buf[:, :], buf[:, :], MAGIC, MAGIC, ALU.add,
                     ALU.subtract, waits=[(SV, K.cnt[SV])])
                vq = v_ts(xqi[:, :], buf[:, :], 127.0, -128.0, ALU.min,
                          ALU.max, waits=[(SV, K.cnt[SV]), (SP, K.cnt[SP])])
                xcons_v[sl] = K.cnt[SV]
                tw = [(SV, vq)]
                if tr_ev:
                    tw.append((SV, tr_ev))
                for b in range(HK):
                    t_tr(ptr[:, b * 128:(b + 1) * 128],
                         xqi[:, b * 128:(b + 1) * 128],
                         waits=(tw if b == 0 else []),
                         inc=(SP if b == HK - 1 else None))
                tr_ev = v_copy(actT_v[:, :, t * 128:(t + 1) * 128],
                               ptr[:, :].rearrange("p (k c) -> p k c", k=HK),
                               waits=[(SP, K.cnt[SP])])
            actT_done = tr_ev

            silu_cnt = {}
            upev_cnt = {}
            h_cnt = {}
            hdma_cnt = {}
            wg_cons = [0, 0]
            for m in range(IT):
                sl = m % 2
                lw = [(SSP, K.cnt[SSP])]
                if wg_cons[sl]:
                    lw.append((SP, wg_cons[sl]))
                K.dma("sync",
                      wgu[sl][:, 0:H].rearrange("p (k c) -> p k c", k=HK),
                      d_gsp[m * HK * 128:(m + 1) * HK * 128, :]
                      .rearrange("(k h) c -> h k c", k=HK), SWG[sl], waits=lw)
                ldu = K.dma("sync",
                            wgu[sl][:, H:2 * H]
                            .rearrange("p (k c) -> p k c", k=HK),
                            d_usp[m * HK * 128:(m + 1) * HK * 128, :]
                            .rearrange("(k h) c -> h k c", k=HK), SWG[sl])
                for tc in range(NTC):
                    gi = m * NTC + tc
                    hg = (gi % 2) * 512
                    pg, pu = pA[:, hg:hg + 512], pB[:, hg:hg + 512]
                    tw = [(SWG[sl], ldu), (SV, actT_done)]
                    if gi >= 2:
                        tw.append((SA, silu_cnt[gi - 2]))
                    for k in range(HK):
                        t_mm(pg, wgu[sl][:, k * 128:(k + 1) * 128],
                             actT_v[:, k, tc * 512:(tc + 1) * 512],
                             k == 0, k == HK - 1,
                             waits=(tw if k == 0 else []))
                    gmm = K.cnt[SP]
                    tw2 = [(SV, upev_cnt[gi - 2])] if gi >= 2 else []
                    for k in range(HK):
                        t_mm(pu,
                             wgu[sl][:, H + k * 128:H + (k + 1) * 128],
                             actT_v[:, k, tc * 512:(tc + 1) * 512],
                             k == 0, k == HK - 1,
                             waits=(tw2 if k == 0 else []))
                    umm = K.cnt[SP]
                    sw_s = [(SP, gmm)]
                    if gi >= 2:
                        sw_s.append((SV, h_cnt[gi - 2]))
                    silu_cnt[gi] = a_act(gs[gi % 2][:, :], pg,
                                         AF.Silu, scale=swxg[:, m:m + 1],
                                         waits=sw_s)
                    uw = [(SP, umm)]
                    if gi >= 2:
                        uw.append((SV, h_cnt[gi - 2]))
                    upev_cnt[gi] = v_ts(up[gi % 2][:, :], pu,
                                        swxu[:, m:m + 1], None, ALU.mult,
                                        waits=uw)
                    hw = [(SA, silu_cnt[gi]), (SV, upev_cnt[gi])]
                    if gi >= 4:
                        hw.append((SHWs[gi % 4], hdma_cnt[gi - 4]))
                    h_cnt[gi] = v_tt(ht[gi % 2][:, :], gs[gi % 2][:, :],
                                     up[gi % 2][:, :], ALU.mult, waits=hw)
                    v_red(hmax_a[:, gi:gi + 1], ht[gi % 2][:, :], ALU.max,
                          absv=True, waits=[(SV, h_cnt[gi])])
                    v_copy(htb[gi % 4][:, :], ht[gi % 2][:, :])
                    hdma_cnt[gi] = K.dma(
                        "gpsimd",
                        d_h[m * 128:(m + 1) * 128, tc * 512:(tc + 1) * 512],
                        htb[gi % 4][:, :], SHWs[gi % 4],
                        waits=[(SV, K.cnt[SV])])
                wg_cons[sl] = K.cnt[SP]

            v_red(colmax[:], hmax_a[:, :], ALU.max, waits=[(SV, K.cnt[SV])])
            dT = K.dma("gpsimd", rowmax[0:1, :], colmax[:, 0:1], STD,
                       waits=[(SV, K.cnt[SV])], gate=True)
            v_red(sc[0:1, 0:1], rowmax[0:1, :], ALU.max, waits=[(STD, dT)])
            d2 = allreduce_max(sc[0:1, 0:1], call * 2 + 1, [(SV, K.cnt[SV])])
            sigmoid_via_exp(sc[0:1, 5:6], clip_s[0:1, 1:2])
            v_tt(sc[0:1, 6:7], sc[0:1, 5:6], gmaxr[0:1, 0:1], ALU.mult,
                 waits=[(STD, d2), (SV, K.cnt[SV])])
            v_recip(sc[0:1, 7:8], sc[0:1, 6:7], waits=[(SV, K.cnt[SV])])
            v_ts(sc[0:1, 7:8], sc[0:1, 7:8], 127.0, None, ALU.mult,
                 waits=[(SV, K.cnt[SV])])
            v_ts(sc[0:1, 8:9], sc[0:1, 6:7], INV127 * 0.5, None, ALU.mult,
                 waits=[(SV, K.cnt[SV])])
            bcast128(bc3[:], sc[0:1, 7:8], waits=[(SV, K.cnt[SV])])
            bcast128(bc4[:], sc[0:1, 8:9])
            hsc_a = K.cnt[SA]

            hf_cons = {}
            sv_at_F = K.cnt[SV]
            hq_cons = [K.cnt[SP], K.cnt[SP]]
            evac_cnt = {}
            ldx = {}
            quant_done = {}
            hfv = tuple(t[:, :] for t in hfb) + tuple(t[:, :] for t in htb)
            hfw_rd = [0]
            SHF8 = SHF4 + SHWF

            def emit_quant_chunk(tc, m0, m1):
                hsl = tc % 2
                for m in range(m0, m1):
                    fsl = m % 8
                    lw = [(s, K.cnt[s]) for s in
                          ("SHWa", "SHWb", "SHWc", "SHWd")]
                    if m >= 8:
                        prev = (tc, m - 8)
                    else:
                        lastm = max(mm2 for mm2 in range(IT)
                                    if mm2 % 8 == m % 8)
                        prev = (tc - 1, lastm)
                    if prev in hf_cons:
                        lw.append((SV, hf_cons[prev]))
                    else:
                        lw.append((SV, sv_at_F))
                    ld = K.dma("sync" if fsl % 2 == 0 else "scalar",
                               hfv[fsl],
                               d_h[m * 128:(m + 1) * 128,
                                   tc * 512:(tc + 1) * 512],
                               SHF8[fsl], waits=lw)
                    qw = [(SHF8[fsl], ld), (SA, hsc_a)]
                    if hq_cons[hsl]:
                        qw.append((SP, hq_cons[hsl]))
                    if hfw_rd[0]:
                        qw.append((SV, hfw_rd[0]))
                    hf_cons[(tc, m)] = v_ts(hfw[:, :], hfv[fsl], bc3[:],
                                            MAGIC, ALU.mult, ALU.add,
                                            waits=qw)
                    v_ts(hfw[:, :], hfw[:, :], MAGIC, 127.0,
                         ALU.subtract, ALU.min, waits=[(SV, K.cnt[SV])])
                    hfw_rd[0] = v_ts(hqs[hsl][:, m * 512:(m + 1) * 512],
                                     hfw[:, :], -128.0, None, ALU.max,
                                     waits=[(SV, K.cnt[SV])])

            def emit_group_tt(tc, tt):
                hsl = tc % 2
                if True:
                    g2 = tc * 4 + tt
                    xsl = g2 % 2
                    xw = [(SV, xcons_v[xsl])]
                    if xcons_a[xsl]:
                        xw.append((SA, xcons_a[xsl]))
                        xcons_a[xsl] = 0
                    ldx[g2] = K.dma("gpsimd", xs[xsl][:, :],
                                    src[g2 * 128:(g2 + 1) * 128, :],
                                    SXG[xsl], waits=xw)
                    for jc in range(2):
                        gidx = tc * 8 + tt * 2 + jc
                        bk = gidx % 2
                        tw = [(SV, quant_done[tc]), (SWL, wdq_ld)]
                        if gidx >= 2:
                            tw.append((SV, evac_cnt[gidx - 2]))
                        for m in range(IT):
                            t_mm(pCD[bk][:, :],
                                 hqs[hsl][:, m * 512 + tt * 128:
                                          m * 512 + (tt + 1) * 128],
                                 wdq[:, m * H + jc * 512:
                                     m * H + (jc + 1) * 512],
                                 m == 0, m == IT - 1,
                                 waits=(tw if m == 0 else []))
                        ow = [(SP, K.cnt[SP]), (SXG[xsl], ldx[g2]),
                              (SA, hsc_a)]
                        if ocons[g2 % 2]:
                            ow.append((SO[g2 % 2], ocons[g2 % 2]))
                        evac_cnt[gidx] = v_stt(
                            xn[g2 % 2][:, jc * 512:(jc + 1) * 512],
                            pCD[bk][:, :], bc4[:],
                            xs[xsl][:, jc * 512:(jc + 1) * 512],
                            ALU.mult, ALU.add, waits=ow)
                    xcons_v[xsl] = K.cnt[SV]
                    ocons[g2 % 2] = K.dma(
                        "gpsimd", dst[g2 * 128:(g2 + 1) * 128, :],
                        xn[g2 % 2][:, :], SO[g2 % 2],
                        waits=[(SV, K.cnt[SV])])

            CH = [(0, 6), (6, 12), (12, 17), (17, 22)]
            for tc in range(NTC + 1):
                for tt in range(4):
                    if tc >= 1:
                        emit_group_tt(tc - 1, tt)
                    if tc < NTC:
                        emit_quant_chunk(tc, CH[tt][0], CH[tt][1])
                if tc < NTC:
                    quant_done[tc] = K.cnt[SV]
                if tc >= 1:
                    hq_cons[(tc - 1) % 2] = K.cnt[SP]
            K.join()
            ffn_st.close()

        ffn(0, d_x, d_xres, skip_A=True, pre_d2=pre_d2_0)

        at_st = contextlib.ExitStack()
        QKT = sb("QKT", [128, 16 * S], BF16, at_st)
        Vpad = sb("Vpad", [128, NTB * NHEADS * 65], BF16, at_st)
        yet = sb("yet", [128, 16384], BF16, at_st)
        ET = (yet[:, 0:8192], yet[:, 8192:16384])
        attnT = sb("attnT", [128, HK * S], BF16, at_st)
        woutb = sb("woutb", [128, HK * 512], BF16, at_st)
        qkw = (sb("qkw0", [128, H], BF16, at_st),
               sb("qkw1", [128, H], BF16, at_st))
        vw0 = sb("vw0", [128, HK * 256], BF16, at_st)
        bc_s = sb("bc_s", [64, 512], BF16, at_st)
        bc_sb = sb("bc_sb", [64, 512], BF16, at_st)
        bc_s2 = (bc_s[:, :], bc_sb[:, :])
        araw = (xn[1][0:65, 0:512], xn[1][0:65, 512:1024])
        recbB0 = sb("recbB0", [1, 512], BF16, at_st)
        recbB1 = sb("recbB1", [1, 512], BF16, at_st)
        recbs = (recbB0[0:1, :], recbB1[0:1, :])
        ones_bf = sb("ones_bf", [1, 128], BF16, at_st)
        ybuf = qkw[0]
        sqb = (sb("sqb0", [128, 512], BF16, at_st),
               sb("sqb1", [128, 512], BF16, at_st))
        cur["sqA"] = xn[0]
        a2_i = [0]
        oc_b0 = []

        a2_ld = {}

        def a2_load(t):
            sl = t % 2
            waits = list(oc_b0 if t < 2 else [])
            if xcons_v[sl]:
                waits.append((SV, xcons_v[sl]))
            if xcons_a[sl]:
                waits.append((SA, xcons_a[sl]))
            a2_ld[t] = K.dma("sync", xs[sl][:, :],
                             d_xres[t * 128:(t + 1) * 128, :],
                             SX[sl], waits=waits)

        def a2_piece(t, phase, piece):
            sl = t % 2
            buf = xs[sl]
            if phase == 0:
                alu = (ALU.add, ALU.max, ALU.min)[piece]
                dst = (S_a, Mx_a, Mn_a)[piece]
                v_red(dst[:, t:t + 1], buf[:, :], alu,
                      waits=[(SX[sl], a2_ld[t])])
            elif piece == 0:
                v_tt(cur["sqA"][:, :], buf[:, :], buf[:, :], ALU.mult,
                     waits=[(SX[sl], a2_ld[t]), (SV, K.cnt[SV])])
            elif piece == 1:
                v_red(Q_a[:, t:t + 1], cur["sqA"][:, :], ALU.add,
                      waits=[(SV, K.cnt[SV])])
                v_ts(m_a[:, t:t + 1], S_a[:, t:t + 1], INV1024, None,
                     ALU.mult, waits=[(SV, K.cnt[SV])])
                v_ts(tmp_a[:, t:t + 1], Q_a[:, t:t + 1], INV1024, None,
                     ALU.mult, waits=[(SV, K.cnt[SV])])
                v_tt(var_a[:, t:t + 1], m_a[:, t:t + 1], m_a[:, t:t + 1],
                     ALU.mult, waits=[(SV, K.cnt[SV])])
                v_tt(var_a[:, t:t + 1], tmp_a[:, t:t + 1], var_a[:, t:t + 1],
                     ALU.subtract, waits=[(SV, K.cnt[SV])])
            else:
                a_act(var_a[:, t:t + 1], var_a[:, t:t + 1], AF.Ln,
                      bias=eps_c[:], waits=[(SV, K.cnt[SV])])
                a_act(r_a[:, t:t + 1], var_a[:, t:t + 1], AF.Exp, scale=-0.5,
                      waits=[(SA, K.cnt[SA])])
                xcons_v[sl] = K.cnt[SV]
                xcons_a[sl] = K.cnt[SA]
                if t + 2 < NTB:
                    a2_load(t + 2)
        Vpad_v = Vpad[:, :].rearrange("p (tt h c) -> p tt h c",
                                      h=NHEADS, c=65)

        vones = K.op("gpsimd",
                     lambda e: e.memset(Vpad_v[:, :, :, 64:65], 1.0), inc=SG)
        vones = K.op("gpsimd", lambda e: e.memset(ones_bf[:], 1.0), inc=SG)
        for eng in ("scalar", "vector"):
            if ocons[0]:
                K.wait(eng, [(SO[0], ocons[0])])
            if ocons[1]:
                K.wait(eng, [(SO[1], ocons[1])])

        for b in range(B_CORE):
            tb = b * NTB
            YT_v = yet[:, :].rearrange("p (k tt) -> p k tt", k=HK)
            tr_ev = 0
            for t in range(NTB):
                g2 = tb + t
                sl = t % 2
                buf = xs[sl]
                waits = [(SV, xcons_v[sl])]
                if xcons_a[sl]:
                    waits.append((SA, xcons_a[sl]))
                ld = K.dma("sync", buf[:, :],
                           d_xres[g2 * 128:(g2 + 1) * 128, :], SX[sl],
                           waits=waits)
                ln_stats_tile(g2, buf, SX[sl], ld, False)
                v_stt(tmp_a[:, g2:g2 + 1], m_a[:, g2:g2 + 1], -1.0,
                      r_a[:, g2:g2 + 1], ALU.mult, ALU.mult,
                      waits=[(SA, K.cnt[SA])])
                ya = a_act(ybuf[:, :], buf[:, :], AF.Identity,
                           bias=tmp_a[:, g2:g2 + 1],
                           scale=r_a[:, g2:g2 + 1],
                           waits=[(SV, K.cnt[SV]), (SP, K.cnt[SP])])
                xcons_v[sl] = K.cnt[SV]
                xcons_a[sl] = ya
                tw = [(SA, ya)]
                if tr_ev:
                    tw.append((SV, tr_ev))
                for bb in range(HK):
                    t_tr(ptr[:, bb * 128:(bb + 1) * 128],
                         ybuf[:, bb * 128:(bb + 1) * 128],
                         waits=(tw if bb == 0 else []),
                         inc=(SP if bb == HK - 1 else None))
                tr_ev = v_copy(YT_v[:, :, t * 128:(t + 1) * 128],
                               ptr[:, :].rearrange("p (k c) -> p k c", k=HK),
                               waits=[(SP, K.cnt[SP])])
            yt_done = tr_ev

            qkv_ev = {}
            qk_cons = [0, 0]
            sq_prev = {}
            red_prev = {}
            svb_start = K.cnt[SV]

            def emit_ssq(pt):
                for qc in range(NQC):
                    ii = pt * NQC + qc
                    ssl = ii % 2
                    sw = []
                    if (ii - 2) in sq_prev:
                        sw.append((SP, sq_prev[ii - 2]))
                    aq = a_act(sqb[ssl][:, :],
                               QKT[:, pt * S + qc * 512:
                                   pt * S + (qc + 1) * 512],
                               AF.Square, waits=sw)
                    mw = [(SA, aq)]
                    if (ii - 1) in red_prev:
                        mw.append((SV, red_prev[ii - 1]))
                    else:
                        mw.append((SV, svb_start))
                        mw.append((SG, K.cnt[SG]))
                    t_mm(pE[0:2, :], ones2[:, :], sqb[ssl][:, :], True,
                         True, waits=mw)
                    sq_prev[ii] = K.cnt[SP]
                    red_prev[ii] = v_red(
                        ssqm[:, ii:ii + 1], pE[0:2, :], ALU.max,
                        waits=[(SP, sq_prev[ii])])

            for c in range(16):
                sl = c % 2
                lw = [(SSP, K.cnt[SSP])]
                if c < 2:
                    lw.append((SP, K.cnt[SP]))
                    lw.append((SA, K.cnt[SA]))
                if qk_cons[sl]:
                    lw.append((SP, qk_cons[sl]))
                ld = K.dma("sync",
                           qkw[sl][:, :].rearrange("p (k c) -> p k c", k=HK),
                           d_qksp[c * HK * 128:(c + 1) * HK * 128, :]
                           .rearrange("(k h) w -> h k w", k=HK),
                           SQK[sl], waits=lw)
                for qc in range(NQC):
                    ii = c * NQC + qc
                    bk = ii % 2
                    tw = [(SQK[sl], ld), (SV, yt_done)]
                    if ii >= 2:
                        tw.append((SA, qkv_ev[ii - 2]))
                    for k in range(HK):
                        t_mm(pCD[bk][:, :], qkw[sl][:, k * 128:(k + 1) * 128],
                             YT_v[:, k, qc * 512:(qc + 1) * 512],
                             k == 0, k == HK - 1,
                             waits=(tw if k == 0 else []))
                    qkv_ev[ii] = a_copy(
                        QKT[:, c * S + qc * 512:c * S + (qc + 1) * 512],
                        pCD[bk][:, :], waits=[(SP, K.cnt[SP])])
                qk_cons[sl] = K.cnt[SP]
                if c >= 2:
                    emit_ssq(c - 2)
            emit_ssq(14)
            emit_ssq(15)
            v_ev = {}
            vsp_v = d_vsp.rearrange("(jc k h) c -> jc h k c", jc=2, k=HK)
            for jc2 in range(4):
                jc, hlf = jc2 // 2, jc2 % 2
                lw = [(SSP, K.cnt[SSP]), (SP, K.cnt[SP])]
                ldv = K.dma("sync",
                            vw0[:, :].rearrange("p (k c) -> p k c", k=HK),
                            vsp_v[jc][:, :, hlf * 256:(hlf + 1) * 256],
                            SVW, waits=lw)
                for tt in range(NTB):
                    bk = tt % 2
                    tw = [(SVW, ldv), (SV, yt_done)]
                    if (jc2, tt - 2) in v_ev:
                        tw.append((SV, v_ev[(jc2, tt - 2)]))
                    elif jc2 > 0:
                        tw.append((SV, v_ev[(jc2 - 1, NTB - 2 + tt)]))
                    elif tt < 2 and (16 * NQC - 2 + tt) in qkv_ev:
                        tw.append((SA, qkv_ev[16 * NQC - 2 + tt]))
                    for k in range(HK):
                        t_mm(pCD[bk][:, 0:256],
                             YT_v[:, k, tt * 128:(tt + 1) * 128],
                             vw0[:, k * 256:(k + 1) * 256],
                             k == 0, k == HK - 1,
                             waits=(tw if k == 0 else []))
                    v_ev[(jc2, tt)] = v_copy(
                        Vpad_v[:, tt, jc2 * 4:(jc2 + 1) * 4, 0:64],
                        pCD[bk][:, 0:256].rearrange("p (h c) -> p h c", c=64),
                        waits=[(SP, K.cnt[SP])])
            v_done = K.cnt[SV]

            red_c = red_prev[16 * NQC - 1]
            v_red(ssqr[:, 0:8],
                  ssqm[0:2, 0:32].rearrange("p (pt qc) -> p pt qc", qc=NQC),
                  ALU.max, waits=[(SV, red_c)])
            v_red(ssqr[:, 8:16],
                  ssqm[0:2, 32:64].rearrange("p (pt qc) -> p pt qc", qc=NQC),
                  ALU.max, waits=[(SV, K.cnt[SV])])
            a_act(ssqr[:, 0:16], ssqr[:, 0:16], AF.Ln,
                  waits=[(SV, K.cnt[SV])])
            a_act(ssqr[:, 0:16], ssqr[:, 0:16], AF.Exp, scale=0.5,
                  waits=[(SA, K.cnt[SA])])
            v_tt(ssqm[0:2, 0:8], ssqr[:, 0:8], ssqr[:, 8:16], ALU.mult,
                 waits=[(SA, K.cnt[SA])])
            v_ts(ssqm[0:2, 0:8], ssqm[0:2, 0:8], -0.125, None, ALU.mult,
                 waits=[(SV, K.cnt[SV])])
            K.dma("gpsimd", negBr[0:1, 0:8], ssqm[0:1, 0:8], STD,
                  waits=[(SV, K.cnt[SV])], gate=True)
            dn2 = K.dma("gpsimd", negBr[0:1, 8:16], ssqm[1:2, 0:8], STD,
                        gate=True)
            t_mm(pE[:, 0:16], ones_row[0:1, 0:128], negBr[0:1, 0:16],
                 True, True,
                 waits=[(STD, dn2), (SA, K.cnt[SA]), (SV, K.cnt[SV])])
            a_copy(negB[:, :].rearrange("p (c2 r) -> p r c2", r=2),
                   pE[:, 0:16].rearrange("p (r c2) -> p r c2", r=2),
                   waits=[(SP, K.cnt[SP])])
            negB_done = K.cnt[SA]
            if b == 1:
                a2_load(0)
                a2_load(1)

            exp_cnt = {}
            pv_mm = {}
            pv_done = {}
            nrm_done = {}
            ep1_cnt = {}
            rec_cnt = {}
            rec2_cnt = {}
            last_rec_rd = [0, 0]
            last_bc_rd = [0, 0]
            last_pe_rd = [0]
            NJ2 = (NHEADS // 2) * NQC

            def heads_of(j2):
                hp, qc = j2 // NQC, j2 % NQC
                return 2 * hp, 2 * hp + 1, hp, qc

            def ep1(j2):
                for hs in range(2):
                    ep1_cnt[(j2, hs)] = v_copy(
                        araw[hs], pCD[hs][0:65, :],
                        waits=[(SP, pv_done[(j2, hs)]), (SV, K.cnt[SV])])
                    rec_cnt[(j2, hs)] = ep1_cnt[(j2, hs)]
                for hs in range(2):
                    rw = [(SV, K.cnt[SV])]
                    if last_rec_rd[hs]:
                        rw.append((SP, last_rec_rd[hs]))
                    v_recip(recb[0:1, :], araw[hs][64:65, :], waits=rw)
                    cw = [(SV, K.cnt[SV])]
                    if last_rec_rd[hs]:
                        cw.append((SP, last_rec_rd[hs]))
                    rec2_cnt[(j2, hs)] = v_copy(recbs[hs], recb[0:1, :],
                                                waits=cw)

            def ep2(j2):
                hA, hB, hp, qc = heads_of(j2)
                for hs in range(2):
                    po = hs * 64
                    bw = [(SV, rec2_cnt[(j2, hs)]), (SA, negB_done)]
                    if last_bc_rd[hs]:
                        bw.append((SV, last_bc_rd[hs]))
                    if last_pe_rd[0]:
                        bw.append((SV, last_pe_rd[0]))
                    t_mm(pE[0:64, :], ones_bf[0:1, 0:64],
                         recbs[hs], True, True, waits=bw)
                    last_rec_rd[hs] = K.cnt[SP]
                    bcp = v_copy(bc_s2[hs], pE[0:64, :],
                                 waits=[(SP, K.cnt[SP])])
                    last_pe_rd[0] = bcp
                    nrm_done[(j2, hs)] = v_tt(
                        attnT[po:po + 64, hp * S + qc * 512:
                              hp * S + (qc + 1) * 512],
                        araw[hs][0:64, :], bc_s2[hs], ALU.mult,
                        waits=[(SV, bcp)])
                    last_bc_rd[hs] = nrm_done[(j2, hs)]

            for x in range(NJ2 + 1):
                curi = x if x < NJ2 else None
                prv = x - 1 if x >= 1 else None
                if curi is not None:
                    chA, chB, chp, cqc = heads_of(curi)
                if prv is not None:
                    phA, phB, php, pqc = heads_of(prv)
                for kt in range(NTB):
                    if prv is not None:
                        for hs, hh in ((0, phA), (1, phB)):
                            tw = ([(SA, exp_cnt[(prv, 1, kt)])]
                                  if hs == 0 else [])
                            if kt == 0:
                                tw += [(SV, v_done), (SG, vones)]
                                if (prv - 1, hs) in rec_cnt:
                                    tw.append((SV, rec_cnt[(prv - 1, hs)]))
                            t_mm(pCD[hs][0:65, :], Vpad_v[:, kt, hh, :],
                                 ET[hs][:, kt * 512:(kt + 1) * 512],
                                 kt == 0, kt == NTB - 1, waits=tw)
                            pv_mm[(prv, hs, kt)] = K.cnt[SP]
                            if kt == NTB - 1:
                                pv_done[(prv, hs)] = K.cnt[SP]
                    if curi is not None:
                        hb = (kt % 2) * 512
                        tw = [(SA, negB_done)]
                        if kt >= 2:
                            tw.append((SA, exp_cnt[(curi, 1, kt - 2)]))
                        elif curi >= 1:
                            tw.append((SA, exp_cnt[(curi - 1, 1,
                                                    NTB - 2 + kt)]))
                        mm_c = {}
                        for hs, pt2, po in ((0, pA, 0), (1, pB, 64)):
                            t_mm(pt2[:, hb:hb + 512],
                                 QKT[po:po + 64,
                                     (8 + chp) * S + kt * 128:
                                     (8 + chp) * S + (kt + 1) * 128],
                                 QKT[po:po + 64,
                                     chp * S + cqc * 512:
                                     chp * S + (cqc + 1) * 512],
                                 True, True, waits=(tw if hs == 0 else []))
                            mm_c[hs] = K.cnt[SP]
                        for hs, pt2, hh in ((0, pA, chA), (1, pB, chB)):
                            ew = [(SP, mm_c[hs])]
                            if prv is not None:
                                ew.append((SP, pv_mm[(prv, hs, kt)]))
                            exp_cnt[(curi, hs, kt)] = a_act(
                                ET[hs][:, kt * 512:(kt + 1) * 512],
                                pt2[:, hb:hb + 512], AF.Exp,
                                bias=negB[:, hh:hh + 1],
                                scale=0.125, waits=ew)
                    if kt == 6 and x >= 2:
                        ep2(x - 2)
                    if kt in (6, 10, 14) and b == 1 and x < 2 * NTB:
                        a2_piece(x // 2, x % 2, {6: 0, 10: 1, 14: 2}[kt])
                if prv is not None:
                    ep1(prv)
            ep2(NJ2 - 1)
            att_done = K.cnt[SV]
            op_ev = {}
            for jc in range(2):
                for k in range(HK):
                    sl = k % 2
                    w = [(SV, xcons_v[sl])]
                    if xcons_a[sl]:
                        w.append((SA, xcons_a[sl]))
                        xcons_a[sl] = 0
                    if jc == 0 and k < 2:
                        w.append((SV, att_done))
                    ldw = K.dma("sync", xs[sl][:, 0:512],
                                d_wout[k * 128:(k + 1) * 128,
                                       jc * 512:(jc + 1) * 512],
                                SX[sl], waits=w)
                    xcons_v[sl] = v_copy(woutb[:, k * 512:(k + 1) * 512],
                                         xs[sl][:, 0:512],
                                         waits=[(SX[sl], ldw)] +
                                         ([(SP, K.cnt[SP])] if jc else []))
                wout_done = K.cnt[SV]
                for qt in range(NTB):
                    g2 = tb + qt
                    xsl = qt % 2
                    xw = [(SV, xcons_v[xsl])]
                    ldx = K.dma("sync", xs[xsl][:, :],
                                d_xres[g2 * 128:(g2 + 1) * 128, :], SX[xsl],
                                waits=xw)
                    gq = jc * NTB + qt
                    bk = gq % 2
                    tw = [(SV, att_done), (SV, wout_done)]
                    if gq >= 2:
                        tw.append((SV, op_ev[gq - 2]))
                    for k in range(HK):
                        t_mm(pCD[bk][:, :],
                             attnT[:, k * S + qt * 128:k * S + (qt + 1) * 128],
                             woutb[:, k * 512:(k + 1) * 512],
                             k == 0, k == HK - 1,
                             waits=(tw if k == 0 else []))
                    ow = [(SP, K.cnt[SP]), (SX[xsl], ldx)]
                    if ocons[xsl]:
                        ow.append((SO[xsl], ocons[xsl]))
                    op_ev[gq] = v_tt(
                        xn[xsl][:, jc * 512:(jc + 1) * 512], pCD[bk][:, :],
                        xs[xsl][:, jc * 512:(jc + 1) * 512], ALU.add,
                        waits=ow)
                    xcons_v[xsl] = K.cnt[SV]
                    ocons[xsl] = K.dma(
                        "gpsimd",
                        d_xres[g2 * 128:(g2 + 1) * 128,
                               jc * 512:(jc + 1) * 512],
                        xn[xsl][:, jc * 512:(jc + 1) * 512], SO[xsl],
                        waits=[(SV, K.cnt[SV])])
            if b == 0:
                oc_b0 = [(SO[i], ocons[i]) for i in range(2) if ocons[i]]
            K.join()

        oc_b1 = [(SO[i], ocons[i]) for i in range(2) if ocons[i]]
        for t in range(NTB, NT):
            ffn_stage_A(d_xres, t, extraw=(oc_b1 if t == NTB else ()))
        pre_d2_1 = stats_tail(2)
        at_st.close()

        ffn(1, d_xres, d_o, skip_A=True, pre_d2=pre_d2_1)

        K.wait("gpsimd", [(SO[0], K.cnt["SO0"]), (SO[1], K.cnt["SO1"])])

        blk = st.enter_context(nc.Block())
        K.realize(blk)
    return nc


def kernel(L0_attn_X, L0_attn_W_qkv, L0_attn_W_out, L0_ffn_clip_x,
           L0_ffn_clip_down, L0_ffn_w_gate, L0_ffn_w_up, L0_ffn_w_down):
    X = np.ascontiguousarray(np.asarray(L0_attn_X, dtype=np.float32))
    wqkv = np.asarray(L0_attn_W_qkv, dtype=np.float32)
    wqkv_p = np.ascontiguousarray(
        wqkv.reshape(H, NHEADS, 64, 3).transpose(0, 3, 1, 2)
        .reshape(H, 3 * H))
    wout = np.ascontiguousarray(np.asarray(L0_attn_W_out, dtype=np.float32))
    wg = np.ascontiguousarray(np.asarray(L0_ffn_w_gate, dtype=np.float32))
    wu = np.ascontiguousarray(np.asarray(L0_ffn_w_up, dtype=np.float32))
    wd = np.ascontiguousarray(np.asarray(L0_ffn_w_down, dtype=np.float32))
    cx = np.ascontiguousarray(
        np.asarray(L0_ffn_clip_x, dtype=np.float32).reshape(1, 1))
    cd = np.ascontiguousarray(
        np.asarray(L0_ffn_clip_down, dtype=np.float32).reshape(1, 1))

    nc = build_program()
    in_maps = []
    for c in range(N_CORES):
        xsh = np.ascontiguousarray(
            X[c * B_CORE:(c + 1) * B_CORE].reshape(T, H))
        in_maps.append(dict(x=xsh, wqkv=wqkv_p, wout=wout, wg=wg, wu=wu,
                            wd=wd, clipx=cx, clipd=cd))
    res = run_bass_kernel_spmd(nc, in_maps, core_ids=list(range(N_CORES)))
    out = np.concatenate(
        [r["o"].reshape(B_CORE, S, H) for r in res.results], axis=0)
    return out
